# version 38
# baseline (speedup 1.0000x reference)
import numpy as np
from contextlib import ExitStack

import concourse.bass as bass
import concourse.mybir as mybir
from concourse.bass_utils import run_bass_kernel_spmd

F32 = mybir.dt.float32
BF16 = mybir.dt.bfloat16
AF = mybir.ActivationFunctionType
ALU = mybir.AluOpType
AX = mybir.AxisListType

D = 1024
TOK = 2048
NTO = 16
NSEQ = 64
NUNITS = 18
NIT = NUNITS * 8
NIT2 = NUNITS * 4
EPS = 1e-5
NEG = -1.0e30

SEMS = ("xin", "xin2", "spill", "wdma", "wdma2", "gp", "pe", "vec", "sca")
DEBUG = False


class Env:
    def __init__(self):
        self.emit = False
        self.counts = {s: 0 for s in SEMS}
        self.ckpts = {}
        self.reqs = set()
        self.sems = {}

    def reset(self):
        self.counts = {s: 0 for s in SEMS}

    def tick(self, sem, key=None, n=1, inst=None):
        c = self.counts[sem] + n
        self.counts[sem] = c
        if key is not None:
            k = (sem, key)
            if self.emit:
                assert self.ckpts[k] == c, (k, self.ckpts[k], c)
            else:
                assert k not in self.ckpts, k
                self.ckpts[k] = c
        if self.emit and inst is not None:
            inst.then_inc(self.sems[sem], n)
        return c

    def at(self, sem, key):
        k = (sem, key)
        if not self.emit:
            self.reqs.add(k)
            return self.ckpts.get(k, 0)
        return self.ckpts[k]

    def wait(self, eng, sem, key):
        v = self.at(sem, key)
        if self.emit and v > 0:
            eng.wait_ge(self.sems[sem], v)

    def vbar(self, eng, inst):
        c = self.tick("vec", inst=inst)
        if self.emit:
            eng.wait_ge(self.sems["vec"], c)

    def check_pass1(self):
        missing = [k for k in self.reqs if k not in self.ckpts]
        assert not missing, f"unresolved ckpts: {sorted(missing)[:10]}"


def _unit_srcs(T, u):
    if u < 2:
        lo, hi = (0, 512) if u == 0 else (512, 1024)
        w1s = T["ws1"][:, :].rearrange("(k p) i -> p k i", p=128)[:, :, lo:hi]
        w3s = T["ws3"][:, :].rearrange("(k p) i -> p k i", p=128)[:, :, lo:hi]
        w2s = T["ws2"][lo:hi, :].rearrange("(k p) d -> p k d", p=128)
    else:
        e = u - 2
        w1s = T["w1"][e].rearrange("(k p) i -> p k i", p=128)
        w3s = T["w3"][e].rearrange("(k p) i -> p k i", p=128)
        w2s = T["w2"][e].rearrange("(k p) d -> p k d", p=128)
    return w1s, w3s, w2s


def emit_sync(env, eng, T):
    E = env.emit
    for t in range(16):
        g = t // 2
        sem = ("xin", "xin2")[g % 2]
        if t % 2 == 0 and g >= 2:
            if E:
                eng.wait_ge(env.sems[sem], env.at(sem, ("Axg", g - 2)))
        dst = T["big"][:, t * D:(t + 1) * D] if E else None
        i = eng.dma_start(out=dst, in_=T["x"][t * 128:(t + 1) * 128, :]) if E else None
        env.tick(sem, n=16, inst=i)
        if t % 2 == 1:
            env.tick(sem, key=("Axg", g), n=0)
    for t in range(NTO):
        if t % 4 == 0 and t > 0 and E:
            eng.wait_ge(env.sems["spill"], env.counts["spill"])
        env.wait(eng, "vec", ("x1", t))
        i = eng.dma_start(out=T["x1spill"][t * 128:(t + 1) * 128, :],
                          in_=T["big"][:, t * D:(t + 1) * D]) if E else None
        env.tick("spill", n=16, inst=i)
        if t % 4 == 3:
            env.tick("spill", key=("spg", t // 4), n=0)
    if DEBUG:
        env.wait(eng, "vec", ("fin", NTO - 1))
        for nm, src in (("dbg_y", T["big"][:, :]), ("dbg_c", T["c_all"][:, :]),
                        ("dbg_rbc", T["rbc"][:, :]), ("dbg_scol", T["scol"][:, :]),
                        ("dbg_x2T", T["x2T"][:, :]), ("dbg_gwt", T["gwt"][:, :]),
                        ("dbg_orow", T["orow"][0:1, :]), ("dbg_vcol", T["vcol"][:, :]),
                        ("dbg_esc", T["esc"][:, :]), ("dbg_scores", T["scores"][:, :])):
            i = eng.dma_start(out=T[nm][:, :], in_=src) if E else None
            env.tick("spill", n=16, inst=i)
    for t in range(NTO):
        osem = ("xin", "xin2")[t % 2]
        if t >= 2 and E:
            eng.wait_ge(env.sems[osem], env.at(osem, ("out", t - 2)))
        env.wait(eng, "vec", ("fin", t))
        i = eng.dma_start(out=T["out"][t * 128:(t + 1) * 128, :],
                          in_=T["outst"][:, (t % 2) * D:(t % 2) * D + D]) if E else None
        env.tick(osem, key=("out", t), n=16, inst=i)
    if E:
        eng.wait_ge(env.sems["spill"], env.counts["spill"])
        eng.wait_ge(env.sems["wdma"], env.counts["wdma"])
        eng.wait_ge(env.sems["wdma2"], env.counts["wdma2"])
        eng.wait_ge(env.sems["xin"], env.counts["xin"])
        eng.wait_ge(env.sems["xin2"], env.counts["xin2"])


def emit_gpsimd(env, eng, T):
    E = env.emit
    if E:
        eng.memset(T["ident"][:, :], 0.0)
    i = eng.affine_select(out=T["ident"][:, :], in_=T["ident"][:, :],
                          compare_op=ALU.not_equal, fill=1.0, base=0,
                          pattern=[[-1, 128]], channel_multiplier=1) if E else None
    env.tick("gp", key="ident", inst=i)
    def _fx2(t):
        fi = t - 16
        g = fi // 4
        sem = ("wdma", "wdma2")[g % 2]
        if fi % 4 == 0 and g >= 4:
            if E:
                eng.wait_ge(env.sems[sem], env.at(sem, ("Afg", g - 4)))
        prev = t - 9
        env.wait(eng, "vec", ("Acp", prev) if prev < 16 else ("Ast", prev))
        env.wait(eng, "pe", ("Amm", prev))
        for u in (t, t + 1):
            i = eng.dma_start(out=T["xff"][:, (u % 10) * D:(u % 10) * D + D],
                              in_=T["x"][u * 128:(u + 1) * 128, :]) if E else None
            env.tick(sem, n=16, inst=i)
        if fi % 4 == 2:
            env.tick(sem, key=("Afg", g), n=0)

    i = eng.dma_start(out=T["gwt"][:, :].rearrange("p (c e) -> p c e", c=8),
                      in_=T["gate_wT"][:, :].rearrange("(c p) e -> p c e", p=128)) if E else None
    env.tick("wdma", key=("w", "gate"), n=16, inst=i)
    i = eng.dma_start(out=T["wu1"][:, :].rearrange("p (c j) -> p c j", c=8),
                      in_=T["wvT"][:, :].rearrange("(c p) j -> p c j", p=128)) if E else None
    env.tick("wdma", key=("w", "wv"), n=16, inst=i)
    i = eng.dma_start(out=T["wu3"][:, :].rearrange("p (c j) -> p c j", c=8),
                      in_=T["woT"][:, :].rearrange("(c p) j -> p c j", p=128)) if E else None
    env.tick("wdma", key=("w", "wo"), n=16, inst=i)
    for t in range(16, NSEQ, 2):
        _fx2(t)
    if E:
        eng.wait_ge(env.sems["wdma"], env.counts["wdma"])
        eng.wait_ge(env.sems["wdma2"], env.counts["wdma2"])
    for u in range(NUNITS):
        p = u % 2
        if E:
            eng.wait_ge(env.sems["wdma"], env.counts["wdma"])
        if u == 0 or u == 1:
            env.wait(eng, "pe", "vsum")
        else:
            env.wait(eng, "pe", ("w2", 4 * ((u - 2) * 4 + 3) + 3))
        w1s, w3s, w2s = _unit_srcs(T, u) if E else (None, None, None)
        i = eng.dma_start(out=T["wu1"][:, p * 4096:(p + 1) * 4096].rearrange("p (k i) -> p k i", k=8),
                          in_=w1s) if E else None
        env.tick("wdma", n=16, inst=i)
        if u == 0 or u == 1:
            env.wait(eng, "pe", "orow")
        i = eng.dma_start(out=T["wu3"][:, p * 4096:(p + 1) * 4096].rearrange("p (k i) -> p k i", k=8),
                          in_=w3s) if E else None
        env.tick("wdma", n=16, inst=i)
        i = eng.dma_start(out=T["wu2"][:, p * 4096:(p + 1) * 4096].rearrange("p (k d) -> p k d", k=4),
                          in_=w2s) if E else None
        env.tick("wdma", key=("w", ("u", u)), n=16, inst=i)
    for t in range(NTO):
        g = t // 4
        rsem = ("wdma", "wdma2")[g % 2]
        if t % 4 == 0 and g >= 2 and E:
            eng.wait_ge(env.sems[rsem], env.at(rsem, ("rdg", g - 2)))
        if t >= 4:
            env.wait(eng, "vec", ("fin", t - 4))
        i = eng.dma_start(out=T["x1rd"][:, (t % 4) * D:(t % 4) * D + D],
                          in_=T["x1spill"][t * 128:(t + 1) * 128, :]) if E else None
        env.tick(rsem, n=16, inst=i)
        if t % 4 == 3:
            env.tick(rsem, key=("rdg", g), n=0)


def emit_tensor(env, eng, T):
    E = env.emit
    P0, P1, P2, P3 = (T["P0"], T["P1"], T["P2"], T["P3"]) if E else (None,) * 4
    PB = (P0, P1, P2, P3) if E else (None,) * 4
    for t in range(NSEQ):
        env.wait(eng, "sca", ("Acast", t))
        if t < 16:
            env.wait(eng, "vec", ("Acp", t))
        elif t % 4 == 0:
            env.wait(eng, ("wdma", "wdma2")[((t - 16) // 4) % 2], ("Afg", (t - 16) // 4))
        for c in range(8):
            dst = PB[c // 2][:, (c % 2) * 512:(c % 2) * 512 + 1] if E else None
            i = eng.matmul(dst,
                           T["xff"][:, (t % 10) * D + c * 128:(t % 10) * D + (c + 1) * 128],
                           T["rcol"][:, t % 4:t % 4 + 1],
                           start=(t == 0), stop=(t == NSEQ - 1)) if E else None
        env.tick("pe", key=("Amm", t), inst=i)
    env.wait(eng, "vec", "maccbf")
    i = eng.matmul(P2[0:1, 300:301], T["maccbf"][:, 0:1], T["onescol"][:, 0:1],
                   start=True, stop=True) if E else None
    env.tick("pe", key="corr1", inst=i)
    env.wait(eng, "vec", "corrbf")
    i = eng.matmul(P3[:, 300:301], T["onesbf"][0:1, 0:128], T["corrbf"][0:1, 0:1],
                   start=True, stop=True) if E else None
    env.tick("pe", key="corrP", inst=i)
    env.wait(eng, "vec", "scol")
    env.wait(eng, "wdma", ("w", "wo"))
    for jc in range(8):
        for dc in range(8):
            dst = PB[jc // 2][:, (jc % 2) * 512:(jc % 2) * 512 + 1] if E else None
            i = eng.matmul(dst,
                           T["wu1"][:, dc * 1024 + jc * 128:dc * 1024 + (jc + 1) * 128],
                           T["scol"][:, dc:dc + 1],
                           start=(dc == 0), stop=(dc == 7)) if E else None
    env.tick("pe", key="vsum", inst=i)
    env.wait(eng, "vec", "vcol")
    env.wait(eng, "wdma", ("w", "wo"))
    for c in range(8):
        for h in range(2):
            i = eng.matmul(P2[0:1, h * 512:(h + 1) * 512],
                           T["vcol"][:, c:c + 1],
                           T["wu3"][:, c * 1024 + h * 512:c * 1024 + (h + 1) * 512],
                           start=(c == 0), stop=(c == 7)) if E else None
    env.tick("pe", key="orow", inst=i)
    env.wait(eng, "vec", "rrow")
    i = eng.matmul(P3[:, 0:512], T["ones_row"][0:1, :], T["orow"][0:1, 0:512],
                   start=True, stop=True) if E else None
    i = eng.matmul(P3[:, 512:1024], T["ones_row"][0:1, :], T["orow"][0:1, 512:1024],
                   start=True, stop=True) if E else None
    env.tick("pe", key="rbc", inst=i)
    pTP = (P0, P1) if E else (None, None)
    pG = (P2, P3) if E else (None, None)
    env.wait(eng, "gp", "ident")
    env.wait(eng, "wdma", ("w", "gate"))
    for t in range(NTO):
        par = t % 2
        env.wait(eng, "vec", ("x2", t))
        if t >= 2:
            env.wait(eng, "vec", ("x2Tevac", t - 2))
            env.wait(eng, "sca", ("x2Te2", t - 2))
        for c in range(8):
            i = eng.matmul(pTP[par][:, c * 128:(c + 1) * 128],
                           T["x2bf"][:, par * D + c * 128:par * D + (c + 1) * 128],
                           T["ident"][:, :], start=True, stop=True) if E else None
        env.tick("pe", key=("tp", t), inst=i)
        env.wait(eng, "vec", ("x2Tevac", t))
        env.wait(eng, "sca", ("x2Te2", t))
        if t >= 2:
            env.wait(eng, "vec", ("gexp_in", t - 2))
        for c in range(8):
            i = eng.matmul(pG[par][:, 0:16],
                           T["x2T"][:, c * TOK + t * 128:c * TOK + (t + 1) * 128],
                           T["gwt"][:, c * 16:(c + 1) * 16],
                           start=(c == 0), stop=(c == 7)) if E else None
        env.tick("pe", key=("gatemm", t), inst=i)
    for it2 in range(NIT2):
        u, ntt = divmod(it2, 4)
        wp = u % 2
        if ntt == 0:
            env.wait(eng, "wdma", ("w", ("u", u)))
        if it2 == 0:
            env.wait(eng, "vec", ("x2Tevac", NTO - 1))
            env.wait(eng, "sca", ("x2Te2", NTO - 1))
            env.wait(eng, "vec", ("gexp_in", NTO - 1))
        if it2 >= 1:
            env.wait(eng, "sca", ("siluB", it2 - 1))
        for m in (0, 1):
            for k in range(8):
                i = eng.matmul(P0[:, m * 512:(m + 1) * 512],
                               T["wu1"][:, wp * 4096 + k * 512 + m * 128:wp * 4096 + k * 512 + (m + 1) * 128],
                               T["x2T"][:, k * TOK + ntt * 512:k * TOK + (ntt + 1) * 512],
                               start=(k == 0), stop=(k == 7)) if E else None
        env.tick("pe", key=("h1A", it2), inst=i)
        if it2 >= 1:
            env.wait(eng, "vec", ("aB", it2 - 1))
        for m in (0, 1):
            for k in range(8):
                i = eng.matmul(P1[:, m * 512:(m + 1) * 512],
                               T["wu3"][:, wp * 4096 + k * 512 + m * 128:wp * 4096 + k * 512 + (m + 1) * 128],
                               T["x2T"][:, k * TOK + ntt * 512:k * TOK + (ntt + 1) * 512],
                               start=(k == 0), stop=(k == 7)) if E else None
        env.tick("pe", key=("h3A", it2), inst=i)
        env.wait(eng, "sca", ("siluA", it2))
        for m in (2, 3):
            for k in range(8):
                i = eng.matmul(P0[:, (m - 2) * 512:(m - 1) * 512],
                               T["wu1"][:, wp * 4096 + k * 512 + m * 128:wp * 4096 + k * 512 + (m + 1) * 128],
                               T["x2T"][:, k * TOK + ntt * 512:k * TOK + (ntt + 1) * 512],
                               start=(k == 0), stop=(k == 7)) if E else None
        env.tick("pe", key=("h1B", it2), inst=i)
        env.wait(eng, "vec", ("aA", it2))
        for m in (2, 3):
            for k in range(8):
                i = eng.matmul(P1[:, (m - 2) * 512:(m - 1) * 512],
                               T["wu3"][:, wp * 4096 + k * 512 + m * 128:wp * 4096 + k * 512 + (m + 1) * 128],
                               T["x2T"][:, k * TOK + ntt * 512:k * TOK + (ntt + 1) * 512],
                               start=(k == 0), stop=(k == 7)) if E else None
        env.tick("pe", key=("h3B", it2), inst=i)
        if it2 >= 1:
            _emit_w2(env, eng, T, it2 - 1)
    _emit_w2(env, eng, T, NIT2 - 1)


def _emit_w2(env, eng, T, prev):
    E = env.emit
    up = prev // 4
    wp = up % 2
    ap = prev % 2
    for blk in range(4):
        g = 4 * prev + blk
        if g >= 2:
            env.wait(eng, "vec", ("yacc", g - 2))
        pY = (T["P2"], T["P3"])[g % 2] if E else None
        for dh in range(2):
            for k in range(4):
                i = eng.matmul(pY[:, dh * 512:(dh + 1) * 512],
                               T["abf"][:, ap * 2048 + k * 512 + blk * 128:ap * 2048 + k * 512 + (blk + 1) * 128],
                               T["wu2"][:, wp * 4096 + k * 1024 + dh * 512:wp * 4096 + k * 1024 + (dh + 1) * 512],
                               start=(k == 0), stop=(k == 3)) if E else None
        env.tick("pe", key=("w2", g), inst=i)


def emit_vector(env, eng, T):
    E = env.emit
    P0, P1, P2, P3 = (T["P0"], T["P1"], T["P2"], T["P3"]) if E else (None,) * 4
    if E:
        eng.memset(T["ones_row"][0:1, :], 1.0)
        eng.memset(T["onesbf"][0:1, :], 1.0)
        eng.memset(T["onescol"][:, 0:1], 1.0)
        eng.memset(T["macc"][:, 0:1], 0.0)
        eng.memset(T["gbuf"][:, 4:8], NEG)
        eng.memset(T["epscol"][:, 0:1], EPS)
    def _Ao_v(t):
        par = t % 4
        xt = T["big"][:, t * D:(t + 1) * D] if E else None
        g = t // 2
        env.wait(eng, ("xin", "xin2")[g % 2], ("Axg", g))
        if t >= 4:
            env.wait(eng, "sca", ("Acast", t - 4))
        i = None
        if E:
            eng.bn_stats(T["bnst"][:, 0:6], xt[:, 0:512])
            i = eng.bn_stats(T["bnst"][:, 6:12], xt[:, 512:1024])
        env.vbar(eng, i)
        i = eng.bn_aggr(T["bnag"][:, 2 * par:2 * par + 2], T["bnst"][:, 0:12]) if E else None
        env.vbar(eng, i)
        i = eng.tensor_scalar(T["smean"][:, par:par + 1], T["bnag"][:, 2 * par:2 * par + 1],
                              -1.0, None, op0=ALU.mult) if E else None
        env.tick("vec", key=("Avar", t), inst=i)
        if E:
            eng.wait_ge(env.sems["vec"], env.counts["vec"])
        if t >= 10:
            env.wait(eng, "pe", ("Amm", t - 10))
        i = eng.tensor_scalar(T["xff"][:, (t % 10) * D:(t % 10 + 1) * D], xt,
                              T["smean"][:, par:par + 1], None, op0=ALU.add) if E else None
        env.tick("vec", key=("Acp", t), inst=i)

    def _Af_stats(t):
        if t % 4 == 0:
            g = (t - 16) // 4
            env.wait(eng, ("wdma", "wdma2")[g % 2], ("Afg", g))
        if t >= 19:
            env.wait(eng, "vec", ("Avar", t - 3))
        par3 = t % 3
        xt = T["xff"][:, (t % 10) * D:(t % 10 + 1) * D] if E else None
        i = None
        if E:
            eng.bn_stats(T["bnst"][:, 12 * par3:12 * par3 + 6], xt[:, 0:512])
            i = eng.bn_stats(T["bnst"][:, 12 * par3 + 6:12 * par3 + 12], xt[:, 512:1024])
        env.tick("vec", key=("Ast", t), inst=i)

    def _Af_aggr(t):
        par = t % 4
        par3 = t % 3
        env.wait(eng, "vec", ("Ast", t))
        env.wait(eng, "sca", ("Acast", t - 4))
        i = eng.bn_aggr(T["bnag"][:, 2 * par:2 * par + 2],
                        T["bnst"][:, 12 * par3:12 * par3 + 12]) if E else None
        env.tick("vec", key=("Avar", t), inst=i)

    def _Af_prod(t):
        par = t % 4
        env.wait(eng, "vec", ("Avar", t))
        env.wait(eng, "sca", ("Acast", t))
        i = eng.scalar_tensor_tensor(T["macc"][:, 0:1], T["bnag"][:, 2 * par:2 * par + 1],
                                     T["rstdf"][:, par:par + 1], T["macc"][:, 0:1],
                                     op0=ALU.mult, op1=ALU.add) if E else None
        env.tick("vec", key=("Apr", t), inst=i)

    for s in range(16):
        _Ao_v(s)
    for j in range(16, NSEQ + 3):
        if j < NSEQ:
            _Af_stats(j)
        if 17 <= j <= NSEQ:
            _Af_aggr(j - 1)
        if 19 <= j <= NSEQ + 2:
            _Af_prod(j - 3)
    i = eng.tensor_copy(T["maccbf"][:, 0:1], T["macc"][:, 0:1]) if E else None
    env.tick("vec", key="maccbf", inst=i)
    env.wait(eng, "pe", ("Amm", NSEQ - 1))
    PBv = (P0, P1, P2, P3) if E else (None,) * 4
    env.wait(eng, "pe", "corr1")
    i = eng.tensor_copy(T["corrbf"][0:1, 0:1], P2[0:1, 300:301]) if E else None
    env.tick("vec", key="corrbf", inst=i)
    env.wait(eng, "pe", "corrP")
    i = eng.tensor_copy(T["corrc"][:, 0:1], P3[:, 300:301]) if E else None
    env.vbar(eng, i)
    i = None
    if E:
        for c in range(8):
            i = eng.tensor_scalar(T["scol"][:, c:c + 1],
                                  PBv[c // 2][:, (c % 2) * 512:(c % 2) * 512 + 1],
                                  T["corrc"][:, 0:1], None, op0=ALU.subtract)
    env.tick("vec", key="scol", inst=i)
    env.wait(eng, "pe", "vsum")
    i = None
    if E:
        for c in range(8):
            i = eng.tensor_copy(T["vcol"][:, c:c + 1],
                                PBv[c // 2][:, (c % 2) * 512:(c % 2) * 512 + 1])
    env.tick("vec", key="vcol", inst=i)
    env.wait(eng, "pe", "orow")
    i = eng.tensor_copy(T["orow"][0:1, :], P2[0:1, 0:1024]) if E else None
    env.vbar(eng, i)
    i = None
    if E:
        eng.bn_stats(T["bnst"][0:1, 0:6], T["orow"][0:1, 0:512])
        i = eng.bn_stats(T["bnst"][0:1, 6:12], T["orow"][0:1, 512:1024])
    env.vbar(eng, i)
    i = eng.bn_aggr(T["bnag"][0:1, 0:2], T["bnst"][0:1, 0:12]) if E else None
    env.vbar(eng, i)
    i = eng.tensor_scalar(T["smean"][0:1, 0:1], T["bnag"][0:1, 0:1],
                          -1.0, None, op0=ALU.mult) if E else None
    env.tick("vec", key="ovar", inst=i)
    if E:
        eng.wait_ge(env.sems["vec"], env.counts["vec"])
    env.wait(eng, "sca", "orstd")
    i = eng.tensor_scalar(T["orow"][0:1, :], T["orow"][0:1, :], T["smean"][0:1, 0:1],
                          T["srstd"][0:1, 0:1], op0=ALU.add, op1=ALU.mult) if E else None
    env.tick("vec", key="rrow", inst=i)
    env.wait(eng, "pe", "rbc")
    i = eng.tensor_copy(T["rbc"][:, :], P3[:, :]) if E else None
    env.tick("vec", key="rbc_sb", inst=i)

    def V1(t):
        big_t = T["big"][:, t * D:(t + 1) * D] if E else None
        i = eng.tensor_tensor(big_t, big_t, T["rbc"][:, :], op=ALU.add) if E else None
        env.tick("vec", key=("x1", t), inst=i)

    def V2(t):
        par = t % 2
        if t >= 2:
            env.wait(eng, "sca", ("B1rstd", t - 2))
        big_t = T["big"][:, t * D:(t + 1) * D] if E else None
        i = None
        if E:
            eng.bn_stats(T["bnst"][:, 0:6], big_t[:, 0:512])
            i = eng.bn_stats(T["bnst"][:, 6:12], big_t[:, 512:1024])
        env.vbar(eng, i)
        i = eng.bn_aggr(T["bnag"][:, 2 * par:2 * par + 2], T["bnst"][:, 0:12]) if E else None
        env.vbar(eng, i)
        i = eng.tensor_scalar(T["smean"][:, par:par + 1], T["bnag"][:, 2 * par:2 * par + 1],
                              -1.0, None, op0=ALU.mult) if E else None
        env.tick("vec", key=("B1var", t), inst=i)
        if E:
            eng.wait_ge(env.sems["vec"], env.counts["vec"])

    def V3(t):
        par = t % 2
        env.wait(eng, "sca", ("B1rstd", t))
        if t >= 2:
            env.wait(eng, "pe", ("tp", t - 2))
        big_t = T["big"][:, t * D:(t + 1) * D] if E else None
        i = eng.tensor_scalar(T["x2bf"][:, par * D:(par + 1) * D], big_t,
                              T["smean"][:, par:par + 1], T["srstd"][:, par:par + 1],
                              op0=ALU.add, op1=ALU.mult) if E else None
        env.tick("vec", key=("x2", t), inst=i)

    def V4(t):
        par = t % 2
        env.wait(eng, "pe", ("tp", t))
        pTPt = (P0, P1)[par] if E else None
        dst = T["x2T"][:, :].rearrange("p (c v) -> p c v", c=8)[:, 0:4, t * 128:(t + 1) * 128] if E else None
        srcv = pTPt.rearrange("p (c v) -> p c v", c=8)[:, 0:4, :] if E else None
        i = eng.tensor_copy(dst, srcv) if E else None
        env.tick("vec", key=("x2Tevac", t), inst=i)

    def V5a(t):
        par = t % 2
        env.wait(eng, "pe", ("gatemm", t))
        if t >= 2:
            env.wait(eng, "sca", ("gexp", t - 2))
        pG = (P2, P3)[par] if E else None
        i = eng.reduce_max(T["m2"][:, 0:1], pG[:, 0:16], axis=AX.X) if E else None
        env.vbar(eng, i)
        i = eng.tensor_scalar(T["scr"][:, par * 16:(par + 1) * 16], pG[:, 0:16],
                              T["m2"][:, 0:1], None, op0=ALU.subtract) if E else None
        env.tick("vec", key=("gexp_in", t), inst=i)

    def V5b(t):
        env.wait(eng, "sca", ("gexp", t))
        esc = T["esc"][:, t * 16:(t + 1) * 16] if E else None
        i = eng.reduce_sum(T["m2"][:, 0:1], esc, axis=AX.X) if E else None
        env.vbar(eng, i)
        i = eng.reciprocal(T["rcp"][:, 0:1], T["m2"][:, 0:1]) if E else None
        env.vbar(eng, i)
        i = None
        if E:
            for g in range(4):
                i = eng.reduce_max(T["gbuf"][:, g:g + 1], esc[:, g * 4:(g + 1) * 4], axis=AX.X)
        env.vbar(eng, i)
        i = eng.max(out=T["gt8"][:, :], in_=T["gbuf"][:, :]) if E else None
        env.vbar(eng, i)
        i = None
        if E:
            for g in range(4):
                i = eng.tensor_scalar(T["gm"][:, g:g + 1], T["gbuf"][:, g:g + 1],
                                      T["gt8"][:, 1:2], None, op0=ALU.is_ge)
        env.vbar(eng, i)
        i = None
        if E:
            for g in range(4):
                i = eng.tensor_scalar(T["masked"][:, g * 4:(g + 1) * 4], esc[:, g * 4:(g + 1) * 4],
                                      T["gm"][:, g:g + 1], None, op0=ALU.mult)
        env.vbar(eng, i)
        i = eng.max(out=T["gt8"][:, :], in_=T["masked"][:, :]) if E else None
        env.vbar(eng, i)
        i = eng.tensor_scalar(T["maske"][:, :], T["masked"][:, :],
                              T["gt8"][:, 3:4], None, op0=ALU.is_ge) if E else None
        env.vbar(eng, i)
        i = eng.tensor_mul(T["masked"][:, :], esc, T["maske"][:, :]) if E else None
        env.vbar(eng, i)
        i = eng.tensor_scalar(T["c_all"][:, t * 16:(t + 1) * 16], T["masked"][:, :],
                              T["rcp"][:, 0:1], None, op0=ALU.mult) if E else None
        env.tick("vec", key=("gate", t), inst=i)

    stages = (V5a, V4, V3, V2, V1)
    for s in range(NTO + 4):
        for j, fn in enumerate(stages):
            t = s - (4 - j)
            if 0 <= t < NTO:
                fn(t)

    def F1(t):
        par = t % 2
        env.wait(eng, "vec", ("yacc", 4 * (17 * 4 + t // 4) + (t % 4)))
        if t >= 2:
            env.wait(eng, "sca", ("Frstd", t - 2))
        yv = T["big"][:, t * D:(t + 1) * D] if E else None
        i = None
        if E:
            eng.bn_stats(T["bnst"][:, 0:6], yv[:, 0:512])
            i = eng.bn_stats(T["bnst"][:, 6:12], yv[:, 512:1024])
        env.vbar(eng, i)
        i = eng.bn_aggr(T["bnag"][:, 2 * par:2 * par + 2], T["bnst"][:, 0:12]) if E else None
        env.vbar(eng, i)
        i = eng.tensor_scalar(T["smean"][:, par:par + 1], T["bnag"][:, 2 * par:2 * par + 1],
                              -1.0, None, op0=ALU.mult) if E else None
        env.tick("vec", key=("Fvar", t), inst=i)
        if E:
            eng.wait_ge(env.sems["vec"], env.counts["vec"])

    def F2(t):
        par = t % 2
        env.wait(eng, "sca", ("Frstd", t))
        if t >= 2:
            env.wait(eng, ("xin", "xin2")[t % 2], ("out", t - 2))
        env.wait(eng, ("wdma", "wdma2")[(t // 4) % 2], ("rdg", t // 4))
        yv = T["big"][:, t * D:(t + 1) * D] if E else None
        ost = T["outst"][:, par * D:(par + 1) * D] if E else None
        i = None
        if E:
            eng.tensor_scalar(ost, yv, T["smean"][:, par:par + 1],
                              T["srstd"][:, par:par + 1], op0=ALU.add, op1=ALU.mult)
            i = eng.tensor_add(ost, ost, T["x1rd"][:, (t % 4) * D:(t % 4) * D + D])
        env.tick("vec", key=("fin", t), inst=i)




    def _yacc(prev):
        up = prev // 4
        for blk in range(4):
            g = 4 * prev + blk
            tj = (prev % 4) * 4 + blk
            env.wait(eng, "pe", ("w2", g))
            pY = (P2, P3)[g % 2] if E else None
            yv = T["big"][:, tj * D:(tj + 1) * D] if E else None
            i = None
            if up == 0:
                env.wait(eng, "spill", ("spg", tj // 4))
                i = eng.tensor_copy(yv, pY[:, :]) if E else None
            elif up == 1:
                i = eng.tensor_add(yv, yv, pY[:, :]) if E else None
            else:
                ccol = T["c_all"][:, tj * 16 + (up - 2):tj * 16 + (up - 2) + 1] if E else None
                i = eng.scalar_tensor_tensor(yv, pY[:, :], ccol, yv,
                                             op0=ALU.mult, op1=ALU.add) if E else None
            env.tick("vec", key=("yacc", g), inst=i)

    fin_steps = []

    def _fin_step(s):
        if 0 <= s - 1 < NTO:
            F2(s - 1)
        if s < NTO:
            F1(s)

    for it2 in range(NIT2):
        ap = it2 % 2
        env.wait(eng, "pe", ("h3A", it2))
        env.wait(eng, "sca", ("siluA", it2))
        if it2 >= 2:
            env.wait(eng, "pe", ("w2", 4 * (it2 - 2) + 3))
        i = eng.tensor_tensor(T["abf"][:, ap * 2048:ap * 2048 + 1024],
                              T["s1"][:, 0:1024], P1[:, :], op=ALU.mult) if E else None
        env.tick("vec", key=("aA", it2), inst=i)
        env.wait(eng, "pe", ("h3B", it2))
        env.wait(eng, "sca", ("siluB", it2))
        i = eng.tensor_tensor(T["abf"][:, ap * 2048 + 1024:ap * 2048 + 2048],
                              T["s1"][:, 1024:2048], P1[:, :], op=ALU.mult) if E else None
        env.tick("vec", key=("aB", it2), inst=i)
        if it2 >= 1:
            _yacc(it2 - 1)
            prev = it2 - 1
            if prev >= 68:
                j = prev - 68
                for q in range(4):
                    _fin_step(4 * j + q)
        if it2 < 8:
            V5b(2 * it2)
            V5b(2 * it2 + 1)
    _yacc(NIT2 - 1)
    for s in range(12, NTO + 1):
        _fin_step(s)

def _xtile(T, t):
    if t < 16:
        return T["big"][:, t * D:(t + 1) * D]
    fi = t - 16
    return T["xff"][:, (fi % 8) * D:(fi % 8) * D + D]


def emit_scalar(env, eng, T):
    E = env.emit
    P0 = T["P0"] if E else None

    def SA2(t):
        par = t % 4
        env.wait(eng, "vec", ("Avar", t))
        if t >= 4:
            env.wait(eng, "pe", ("Amm", t - 4))
        i = None
        if E:
            eng.activation(T["lncol"][:, 0:1], T["bnag"][:, 2 * par + 1:2 * par + 2],
                           AF.Ln, bias=T["epscol"][:, 0:1])
            i = eng.activation(T["rcol"][:, par:par + 1], T["lncol"][:, 0:1], AF.Exp, scale=-0.5)
            if t >= 16:
                i = eng.activation(T["rstdf"][:, par:par + 1], T["lncol"][:, 0:1],
                                   AF.Exp, scale=-0.5)
        env.tick("sca", key=("Acast", t), inst=i)

    for s in range(NSEQ):
        SA2(s)
    env.wait(eng, "vec", "ovar")
    i = None
    if E:
        eng.activation(T["lncol"][0:1, 0:1], T["bnag"][0:1, 1:2], AF.Ln,
                       bias=T["epscol"][0:1, 0:1])
        i = eng.activation(T["srstd"][0:1, 0:1], T["lncol"][0:1, 0:1], AF.Exp, scale=-0.5)
    env.tick("sca", key="orstd", inst=i)

    def S2(t):
        par = t % 2
        env.wait(eng, "vec", ("B1var", t))
        i = None
        if E:
            eng.activation(T["lncol"][:, 0:1], T["bnag"][:, 2 * par + 1:2 * par + 2],
                           AF.Ln, bias=T["epscol"][:, 0:1])
            i = eng.activation(T["srstd"][:, par:par + 1], T["lncol"][:, 0:1], AF.Exp, scale=-0.5)
        env.tick("sca", key=("B1rstd", t), inst=i)

    def SV4(t):
        par = t % 2
        env.wait(eng, "pe", ("tp", t))
        pTPt = (T["P0"], T["P1"])[par] if E else None
        i = None
        if E:
            for c in range(4, 8):
                i = eng.activation(T["x2T"][:, c * TOK + t * 128:c * TOK + (t + 1) * 128],
                                   pTPt[:, c * 128:(c + 1) * 128], AF.Copy, bias=0.0)
        env.tick("sca", key=("x2Te2", t), inst=i)

    def S3(t):
        par = t % 2
        env.wait(eng, "vec", ("gexp_in", t))
        i = eng.activation(T["esc"][:, t * 16:(t + 1) * 16],
                           T["scr"][:, par * 16:(par + 1) * 16], AF.Exp) if E else None
        env.tick("sca", key=("gexp", t), inst=i)

    for s in range(NTO + 4):
        for fn, lag in ((S3, 4), (SV4, 3), (S2, 0)):
            t = s - lag
            if 0 <= t < NTO:
                fn(t)

    def FS2(t):
        par = t % 2
        env.wait(eng, "vec", ("Fvar", t))
        i = None
        if E:
            eng.activation(T["lncol"][:, 0:1], T["bnag"][:, 2 * par + 1:2 * par + 2],
                           AF.Ln, bias=T["epscol"][:, 0:1])
            i = eng.activation(T["srstd"][:, par:par + 1], T["lncol"][:, 0:1], AF.Exp, scale=-0.5)
        env.tick("sca", key=("Frstd", t), inst=i)




    def _fs_step(s):
        if 0 <= s < NTO:
            FS2(s)

    for it2 in range(NIT2):
        env.wait(eng, "pe", ("h1A", it2))
        if it2 >= 1:
            env.wait(eng, "vec", ("aA", it2 - 1))
        i = eng.activation(T["s1"][:, 0:1024], P0[:, :], AF.Silu) if E else None
        env.tick("sca", key=("siluA", it2), inst=i)
        env.wait(eng, "pe", ("h1B", it2))
        if it2 >= 1:
            env.wait(eng, "vec", ("aB", it2 - 1))
        i = eng.activation(T["s1"][:, 1024:2048], P0[:, :], AF.Silu) if E else None
        env.tick("sca", key=("siluB", it2), inst=i)
        if it2 >= 69:
            j = it2 - 69
            for q in range(4):
                _fs_step(4 * j + q)
    for s in range(12, NTO):
        _fs_step(s)

def build_program():
    nc = bass.Bass("TRN2", target_bir_lowering=False, dynamic_dma_scratch_size=16384,
                   detect_race_conditions=False)
    T = {}
    T["x"] = nc.dram_tensor("x", [8192, D], F32, kind="ExternalInput")
    T["wvT"] = nc.dram_tensor("wvT", [D, D], F32, kind="ExternalInput")
    T["woT"] = nc.dram_tensor("woT", [D, D], F32, kind="ExternalInput")
    T["gate_wT"] = nc.dram_tensor("gate_wT", [D, 16], F32, kind="ExternalInput")
    T["w1"] = nc.dram_tensor("w1", [16, D, 512], F32, kind="ExternalInput")
    T["w2"] = nc.dram_tensor("w2", [16, 512, D], F32, kind="ExternalInput")
    T["w3"] = nc.dram_tensor("w3", [16, D, 512], F32, kind="ExternalInput")
    T["ws1"] = nc.dram_tensor("ws1", [D, D], F32, kind="ExternalInput")
    T["ws2"] = nc.dram_tensor("ws2", [D, D], F32, kind="ExternalInput")
    T["ws3"] = nc.dram_tensor("ws3", [D, D], F32, kind="ExternalInput")
    T["out"] = nc.dram_tensor("out", [TOK, D], F32, kind="ExternalOutput")
    T["x1spill"] = nc.dram_tensor("x1spill", [TOK, D], F32,
                                  kind="ExternalOutput" if DEBUG else "Internal")
    if DEBUG:
        T["dbg_y"] = nc.dram_tensor("dbg_y", [128, NTO * D], F32, kind="ExternalOutput")
        T["dbg_c"] = nc.dram_tensor("dbg_c", [128, NTO * 16], F32, kind="ExternalOutput")
        T["dbg_rbc"] = nc.dram_tensor("dbg_rbc", [128, D], F32, kind="ExternalOutput")
        T["dbg_scol"] = nc.dram_tensor("dbg_scol", [128, 8], BF16, kind="ExternalOutput")
        T["dbg_x2T"] = nc.dram_tensor("dbg_x2T", [128, 8 * TOK], BF16, kind="ExternalOutput")
        T["dbg_gwt"] = nc.dram_tensor("dbg_gwt", [128, 8 * 16], BF16, kind="ExternalOutput")
        T["dbg_orow"] = nc.dram_tensor("dbg_orow", [1, D], F32, kind="ExternalOutput")
        T["dbg_vcol"] = nc.dram_tensor("dbg_vcol", [128, 8], BF16, kind="ExternalOutput")
        T["dbg_esc"] = nc.dram_tensor("dbg_esc", [128, 32], F32, kind="ExternalOutput")
        T["dbg_scores"] = nc.dram_tensor("dbg_scores", [128, 16], F32, kind="ExternalOutput")

    with ExitStack() as ctx:
        e = ctx.enter_context

        def sb(name, shape, dt):
            h = e(nc.sbuf_tensor(name, shape, dt))
            T[name] = h
            return h

        sb("big", [128, NTO * D], F32)
        sb("x2T", [128, 8 * TOK], BF16)
        sb("wu1", [128, 2 * 4096], BF16)
        sb("wu3", [128, 2 * 4096], BF16)
        sb("wu2", [128, 2 * 4096], BF16)
        sb("gwt", [128, 8 * 16], BF16)
        sb("xff", [128, 10 * D], BF16)
        sb("x1rd", [128, 4 * D], BF16)
        sb("s1", [128, 2 * D], BF16)
        sb("abf", [128, 2 * 2048], BF16)
        sb("c_all", [128, NTO * 16], F32)
        sb("x2bf", [128, 2 * D], BF16)
        sb("ident", [128, 128], BF16)
        sb("outst", [128, 2 * D], F32)
        sb("rcp", [128, 1], F32)
        sb("bnst", [128, 36], F32)
        sb("bnag", [128, 8], F32)
        sb("rstdf", [128, 4], F32)
        sb("prodc", [128, 1], F32)
        sb("macc", [128, 1], F32)
        sb("maccbf", [128, 1], BF16)
        sb("corrc", [128, 1], F32)
        sb("corrbf", [1, 1], BF16)
        sb("onesbf", [1, 128], BF16)
        sb("onescol", [128, 1], BF16)
        sb("epscol", [128, 1], F32)
        sb("rbc", [128, D], BF16)
        sb("smean", [128, 4], F32)
        sb("srstd", [128, 4], F32)
        sb("m2", [128, 1], F32)
        sb("lncol", [128, 1], F32)
        sb("rcol", [128, 4], BF16)
        sb("scol", [128, 8], BF16)
        sb("vcol", [128, 8], BF16)
        sb("orow", [1, D], F32)
        sb("ones_row", [1, 128], F32)
        sb("scr", [128, 2 * 16], F32)
        sb("esc", [128, 16 * 16], F32)
        sb("gbuf", [128, 8], F32)
        sb("gt8", [128, 8], F32)
        sb("gm", [128, 4], F32)
        sb("masked", [128, 16], F32)
        sb("maske", [128, 16], F32)

        T["P0"] = e(nc.psum_tensor("P0", [128, 1024], F32))
        T["P1"] = e(nc.psum_tensor("P1", [128, 1024], F32))
        T["P2"] = e(nc.psum_tensor("P2", [128, 1024], F32))
        T["P3"] = e(nc.psum_tensor("P3", [128, 1024], F32))

        env = Env()
        emit_sync(env, None, T)
        emit_gpsimd(env, None, T)
        emit_tensor(env, None, T)
        emit_vector(env, None, T)
        emit_scalar(env, None, T)
        env.check_pass1()
        pass1_counts = dict(env.counts)

        env.emit = True
        env.reset()
        for s in SEMS:
            env.sems[s] = e(nc.semaphore(s))
        block = e(nc.Block())
        block.sync(lambda eng: emit_sync(env, eng, T))
        block.gpsimd(lambda eng: emit_gpsimd(env, eng, T))
        block.tensor(lambda eng: emit_tensor(env, eng, T))
        block.vector(lambda eng: emit_vector(env, eng, T))
        block.scalar(lambda eng: emit_scalar(env, eng, T))
        assert env.counts == pass1_counts, (env.counts, pass1_counts)
    return nc


_NC_CACHE = []


def kernel(L0_attn_x, L0_attn_w_qkv, L0_attn_w_o, L0_ffn_gate_w, L0_ffn_w1,
           L0_ffn_w2, L0_ffn_w3, L0_ffn_ws1, L0_ffn_ws2, L0_ffn_ws3):
    x = np.ascontiguousarray(np.asarray(L0_attn_x, dtype=np.float32))
    wvT = np.ascontiguousarray(np.asarray(L0_attn_w_qkv, dtype=np.float32)[2048:3072, :].T)
    woT = np.ascontiguousarray(np.asarray(L0_attn_w_o, dtype=np.float32).T)
    gate_wT = np.ascontiguousarray(np.asarray(L0_ffn_gate_w, dtype=np.float32).T)
    w1 = np.ascontiguousarray(np.asarray(L0_ffn_w1, dtype=np.float32))
    w2 = np.ascontiguousarray(np.asarray(L0_ffn_w2, dtype=np.float32))
    w3 = np.ascontiguousarray(np.asarray(L0_ffn_w3, dtype=np.float32))
    ws1 = np.ascontiguousarray(np.asarray(L0_ffn_ws1, dtype=np.float32))
    ws2 = np.ascontiguousarray(np.asarray(L0_ffn_ws2, dtype=np.float32))
    ws3 = np.ascontiguousarray(np.asarray(L0_ffn_ws3, dtype=np.float32))

    if not _NC_CACHE:
        _NC_CACHE.append(build_program())
    nc = _NC_CACHE[0]

    in_maps = []
    for core in range(8):
        b, q = divmod(core, 4)
        xb = x[b]
        x_seq = np.concatenate(
            [xb[q * TOK:(q + 1) * TOK], xb[:q * TOK], xb[(q + 1) * TOK:]], axis=0)
        in_maps.append({
            "x": np.ascontiguousarray(x_seq),
            "wvT": wvT, "woT": woT, "gate_wT": gate_wT,
            "w1": w1, "w2": w2, "w3": w3,
            "ws1": ws1, "ws2": ws2, "ws3": ws3,
        })
    res = run_bass_kernel_spmd(nc, in_maps, core_ids=list(range(8)))
    out = np.empty((2, 8192, D), dtype=np.float32)
    for core in range(8):
        b, q = divmod(core, 4)
        out[b, q * TOK:(q + 1) * TOK] = np.asarray(res.results[core]["out"])
    return out


# version 39
# speedup vs baseline: 1.0074x; 1.0074x over previous
import numpy as np
from contextlib import ExitStack

import concourse.bass as bass
import concourse.mybir as mybir
from concourse.bass_utils import run_bass_kernel_spmd

F32 = mybir.dt.float32
BF16 = mybir.dt.bfloat16
AF = mybir.ActivationFunctionType
ALU = mybir.AluOpType
AX = mybir.AxisListType

D = 1024
TOK = 2048
NTO = 16
NSEQ = 64
NUNITS = 18
NIT = NUNITS * 8
NIT2 = NUNITS * 4
EPS = 1e-5
NEG = -1.0e30

SEMS = ("xin", "xin2", "spill", "wdma", "wdma2", "gp", "pe", "vec", "sca")
DEBUG = False


class Env:
    def __init__(self):
        self.emit = False
        self.counts = {s: 0 for s in SEMS}
        self.ckpts = {}
        self.reqs = set()
        self.sems = {}

    def reset(self):
        self.counts = {s: 0 for s in SEMS}

    def tick(self, sem, key=None, n=1, inst=None):
        c = self.counts[sem] + n
        self.counts[sem] = c
        if key is not None:
            k = (sem, key)
            if self.emit:
                assert self.ckpts[k] == c, (k, self.ckpts[k], c)
            else:
                assert k not in self.ckpts, k
                self.ckpts[k] = c
        if self.emit and inst is not None:
            inst.then_inc(self.sems[sem], n)
        return c

    def at(self, sem, key):
        k = (sem, key)
        if not self.emit:
            self.reqs.add(k)
            return self.ckpts.get(k, 0)
        return self.ckpts[k]

    def wait(self, eng, sem, key):
        v = self.at(sem, key)
        if self.emit and v > 0:
            eng.wait_ge(self.sems[sem], v)

    def vbar(self, eng, inst):
        c = self.tick("vec", inst=inst)
        if self.emit:
            eng.wait_ge(self.sems["vec"], c)

    def check_pass1(self):
        missing = [k for k in self.reqs if k not in self.ckpts]
        assert not missing, f"unresolved ckpts: {sorted(missing)[:10]}"


def _unit_srcs(T, u):
    if u < 2:
        lo, hi = (0, 512) if u == 0 else (512, 1024)
        w1s = T["ws1"][:, :].rearrange("(k p) i -> p k i", p=128)[:, :, lo:hi]
        w3s = T["ws3"][:, :].rearrange("(k p) i -> p k i", p=128)[:, :, lo:hi]
        w2s = T["ws2"][lo:hi, :].rearrange("(k p) d -> p k d", p=128)
    else:
        e = u - 2
        w1s = T["w1"][e].rearrange("(k p) i -> p k i", p=128)
        w3s = T["w3"][e].rearrange("(k p) i -> p k i", p=128)
        w2s = T["w2"][e].rearrange("(k p) d -> p k d", p=128)
    return w1s, w3s, w2s


def emit_sync(env, eng, T):
    E = env.emit
    for t in range(16):
        g = t // 2
        sem = ("xin", "xin2")[g % 2]
        if t % 2 == 0 and g >= 2:
            if E:
                eng.wait_ge(env.sems[sem], env.at(sem, ("Axg", g - 2)))
        dst = T["big"][:, t * D:(t + 1) * D] if E else None
        i = eng.dma_start(out=dst, in_=T["x"][t * 128:(t + 1) * 128, :]) if E else None
        env.tick(sem, n=16, inst=i)
        if t % 2 == 1:
            env.tick(sem, key=("Axg", g), n=0)
    for t in range(NTO):
        if t % 4 == 0 and t > 0 and E:
            eng.wait_ge(env.sems["spill"], env.counts["spill"])
        env.wait(eng, "vec", ("x1", t))
        i = eng.dma_start(out=T["x1spill"][t * 128:(t + 1) * 128, :],
                          in_=T["big"][:, t * D:(t + 1) * D]) if E else None
        env.tick("spill", n=16, inst=i)
        if t % 4 == 3:
            env.tick("spill", key=("spg", t // 4), n=0)
    if DEBUG:
        env.wait(eng, "vec", ("fin", NTO - 1))
        for nm, src in (("dbg_y", T["big"][:, :]), ("dbg_c", T["c_all"][:, :]),
                        ("dbg_rbc", T["rbc"][:, :]), ("dbg_scol", T["scol"][:, :]),
                        ("dbg_x2T", T["x2T"][:, :]), ("dbg_gwt", T["gwt"][:, :]),
                        ("dbg_orow", T["orow"][0:1, :]), ("dbg_vcol", T["vcol"][:, :]),
                        ("dbg_esc", T["esc"][:, :]), ("dbg_scores", T["scores"][:, :])):
            i = eng.dma_start(out=T[nm][:, :], in_=src) if E else None
            env.tick("spill", n=16, inst=i)
    for t in range(NTO):
        osem = ("xin", "xin2")[t % 2]
        if t >= 2 and E:
            eng.wait_ge(env.sems[osem], env.at(osem, ("out", t - 2)))
        env.wait(eng, "vec", ("fin", t))
        i = eng.dma_start(out=T["out"][t * 128:(t + 1) * 128, :],
                          in_=T["outst"][:, (t % 2) * D:(t % 2) * D + D]) if E else None
        env.tick(osem, key=("out", t), n=16, inst=i)
    if E:
        eng.wait_ge(env.sems["spill"], env.counts["spill"])
        eng.wait_ge(env.sems["wdma"], env.counts["wdma"])
        eng.wait_ge(env.sems["wdma2"], env.counts["wdma2"])
        eng.wait_ge(env.sems["xin"], env.counts["xin"])
        eng.wait_ge(env.sems["xin2"], env.counts["xin2"])


def emit_gpsimd(env, eng, T):
    E = env.emit
    if E:
        eng.memset(T["ident"][:, :], 0.0)
    i = eng.affine_select(out=T["ident"][:, :], in_=T["ident"][:, :],
                          compare_op=ALU.not_equal, fill=1.0, base=0,
                          pattern=[[-1, 128]], channel_multiplier=1) if E else None
    env.tick("gp", key="ident", inst=i)
    def _fx2(t):
        fi = t - 16
        g = fi // 4
        sem = ("wdma", "wdma2")[g % 2]
        if fi % 4 == 0 and g >= 4:
            if E:
                eng.wait_ge(env.sems[sem], env.at(sem, ("Afg", g - 4)))
        prev = t - 9
        env.wait(eng, "vec", ("Acp", prev) if prev < 16 else ("Ast", prev))
        env.wait(eng, "pe", ("Amm", prev))
        for u in (t, t + 1):
            i = eng.dma_start(out=T["xff"][:, (u % 10) * D:(u % 10) * D + D],
                              in_=T["x"][u * 128:(u + 1) * 128, :]) if E else None
            env.tick(sem, n=16, inst=i)
        if fi % 4 == 2:
            env.tick(sem, key=("Afg", g), n=0)

    i = eng.dma_start(out=T["gwt"][:, :].rearrange("p (c e) -> p c e", c=8),
                      in_=T["gate_wT"][:, :].rearrange("(c p) e -> p c e", p=128)) if E else None
    env.tick("wdma", key=("w", "gate"), n=16, inst=i)
    i = eng.dma_start(out=T["wu1"][:, :].rearrange("p (c j) -> p c j", c=8),
                      in_=T["wvT"][:, :].rearrange("(c p) j -> p c j", p=128)) if E else None
    env.tick("wdma", key=("w", "wv"), n=16, inst=i)
    i = eng.dma_start(out=T["wu3"][:, :].rearrange("p (c j) -> p c j", c=8),
                      in_=T["woT"][:, :].rearrange("(c p) j -> p c j", p=128)) if E else None
    env.tick("wdma", key=("w", "wo"), n=16, inst=i)
    for t in range(16, NSEQ, 2):
        _fx2(t)
    if E:
        eng.wait_ge(env.sems["wdma"], env.counts["wdma"])
        eng.wait_ge(env.sems["wdma2"], env.counts["wdma2"])
    for u in range(NUNITS):
        p = u % 2
        if E:
            eng.wait_ge(env.sems["wdma"], env.counts["wdma"])
        if u == 0 or u == 1:
            env.wait(eng, "pe", "vsum")
        else:
            env.wait(eng, "pe", ("w2", 4 * ((u - 2) * 4 + 3) + 3))
        w1s, w3s, w2s = _unit_srcs(T, u) if E else (None, None, None)
        i = eng.dma_start(out=T["wu1"][:, p * 4096:(p + 1) * 4096].rearrange("p (k i) -> p k i", k=8),
                          in_=w1s) if E else None
        env.tick("wdma", n=16, inst=i)
        if u == 0 or u == 1:
            env.wait(eng, "pe", "orow")
        i = eng.dma_start(out=T["wu3"][:, p * 4096:(p + 1) * 4096].rearrange("p (k i) -> p k i", k=8),
                          in_=w3s) if E else None
        env.tick("wdma", n=16, inst=i)
        i = eng.dma_start(out=T["wu2"][:, p * 4096:(p + 1) * 4096].rearrange("p (k d) -> p k d", k=4),
                          in_=w2s) if E else None
        env.tick("wdma", key=("w", ("u", u)), n=16, inst=i)
    for t in range(NTO):
        g = t // 4
        rsem = ("wdma", "wdma2")[g % 2]
        if t % 4 == 0 and g >= 2 and E:
            eng.wait_ge(env.sems[rsem], env.at(rsem, ("rdg", g - 2)))
        if t >= 4:
            env.wait(eng, "vec", ("fin", t - 4))
        i = eng.dma_start(out=T["x1rd"][:, (t % 4) * D:(t % 4) * D + D],
                          in_=T["x1spill"][t * 128:(t + 1) * 128, :]) if E else None
        env.tick(rsem, n=16, inst=i)
        if t % 4 == 3:
            env.tick(rsem, key=("rdg", g), n=0)


def emit_tensor(env, eng, T):
    E = env.emit
    P0, P1, P2, P3 = (T["P0"], T["P1"], T["P2"], T["P3"]) if E else (None,) * 4
    PB = (P0, P1, P2, P3) if E else (None,) * 4
    for t in range(NSEQ):
        env.wait(eng, "sca", ("Acast", t))
        if t < 16:
            env.wait(eng, "vec", ("Acp", t))
        elif t % 4 == 0:
            env.wait(eng, ("wdma", "wdma2")[((t - 16) // 4) % 2], ("Afg", (t - 16) // 4))
        for c in range(8):
            dst = PB[c // 2][:, (c % 2) * 512:(c % 2) * 512 + 1] if E else None
            i = eng.matmul(dst,
                           T["xff"][:, (t % 10) * D + c * 128:(t % 10) * D + (c + 1) * 128],
                           T["rcol"][:, t % 4:t % 4 + 1],
                           start=(t == 0), stop=(t == NSEQ - 1)) if E else None
        env.tick("pe", key=("Amm", t), inst=i)
    env.wait(eng, "vec", "maccbf")
    i = eng.matmul(P2[0:1, 300:301], T["maccbf"][:, 0:1], T["onescol"][:, 0:1],
                   start=True, stop=True) if E else None
    env.tick("pe", key="corr1", inst=i)
    env.wait(eng, "vec", "corrbf")
    i = eng.matmul(P3[:, 300:301], T["onesbf"][0:1, 0:128], T["corrbf"][0:1, 0:1],
                   start=True, stop=True) if E else None
    env.tick("pe", key="corrP", inst=i)
    env.wait(eng, "vec", "scol")
    env.wait(eng, "wdma", ("w", "wo"))
    for jc in range(8):
        for dc in range(8):
            dst = PB[jc // 2][:, (jc % 2) * 512:(jc % 2) * 512 + 1] if E else None
            i = eng.matmul(dst,
                           T["wu1"][:, dc * 1024 + jc * 128:dc * 1024 + (jc + 1) * 128],
                           T["scol"][:, dc:dc + 1],
                           start=(dc == 0), stop=(dc == 7)) if E else None
    env.tick("pe", key="vsum", inst=i)
    env.wait(eng, "vec", "vcol")
    env.wait(eng, "wdma", ("w", "wo"))
    for c in range(8):
        for h in range(2):
            i = eng.matmul(P2[0:1, h * 512:(h + 1) * 512],
                           T["vcol"][:, c:c + 1],
                           T["wu3"][:, c * 1024 + h * 512:c * 1024 + (h + 1) * 512],
                           start=(c == 0), stop=(c == 7)) if E else None
    env.tick("pe", key="orow", inst=i)
    env.wait(eng, "vec", "rrow")
    i = eng.matmul(P3[:, 0:512], T["ones_row"][0:1, :], T["orow"][0:1, 0:512],
                   start=True, stop=True) if E else None
    i = eng.matmul(P3[:, 512:1024], T["ones_row"][0:1, :], T["orow"][0:1, 512:1024],
                   start=True, stop=True) if E else None
    env.tick("pe", key="rbc", inst=i)
    pTP = (P0, P1) if E else (None, None)
    pG = (P2, P3) if E else (None, None)
    env.wait(eng, "gp", "ident")
    env.wait(eng, "wdma", ("w", "gate"))
    for t in range(NTO):
        par = t % 2
        env.wait(eng, "vec", ("x2", t))
        if t >= 2:
            env.wait(eng, "vec", ("x2Tevac", t - 2))
        for c in range(8):
            i = eng.matmul(pTP[par][:, c * 128:(c + 1) * 128],
                           T["x2bf"][:, par * D + c * 128:par * D + (c + 1) * 128],
                           T["ident"][:, :], start=True, stop=True) if E else None
        env.tick("pe", key=("tp", t), inst=i)
        env.wait(eng, "vec", ("x2Tevac", t))
        if t >= 2:
            env.wait(eng, "vec", ("gexp_in", t - 2))
        for c in range(8):
            i = eng.matmul(pG[par][:, 0:16],
                           T["x2T"][:, c * TOK + t * 128:c * TOK + (t + 1) * 128],
                           T["gwt"][:, c * 16:(c + 1) * 16],
                           start=(c == 0), stop=(c == 7)) if E else None
        env.tick("pe", key=("gatemm", t), inst=i)
    for it2 in range(NIT2):
        u, ntt = divmod(it2, 4)
        wp = u % 2
        if ntt == 0:
            env.wait(eng, "wdma", ("w", ("u", u)))
        if it2 == 0:
            env.wait(eng, "vec", ("x2Tevac", NTO - 1))
            env.wait(eng, "vec", ("gexp_in", NTO - 1))
        if it2 >= 1:
            env.wait(eng, "sca", ("siluB", it2 - 1))
        for m in (0, 1):
            for k in range(8):
                i = eng.matmul(P0[:, m * 512:(m + 1) * 512],
                               T["wu1"][:, wp * 4096 + k * 512 + m * 128:wp * 4096 + k * 512 + (m + 1) * 128],
                               T["x2T"][:, k * TOK + ntt * 512:k * TOK + (ntt + 1) * 512],
                               start=(k == 0), stop=(k == 7)) if E else None
        env.tick("pe", key=("h1A", it2), inst=i)
        if it2 >= 1:
            env.wait(eng, "vec", ("aB", it2 - 1))
        for m in (0, 1):
            for k in range(8):
                i = eng.matmul(P1[:, m * 512:(m + 1) * 512],
                               T["wu3"][:, wp * 4096 + k * 512 + m * 128:wp * 4096 + k * 512 + (m + 1) * 128],
                               T["x2T"][:, k * TOK + ntt * 512:k * TOK + (ntt + 1) * 512],
                               start=(k == 0), stop=(k == 7)) if E else None
        env.tick("pe", key=("h3A", it2), inst=i)
        env.wait(eng, "sca", ("siluA", it2))
        for m in (2, 3):
            for k in range(8):
                i = eng.matmul(P0[:, (m - 2) * 512:(m - 1) * 512],
                               T["wu1"][:, wp * 4096 + k * 512 + m * 128:wp * 4096 + k * 512 + (m + 1) * 128],
                               T["x2T"][:, k * TOK + ntt * 512:k * TOK + (ntt + 1) * 512],
                               start=(k == 0), stop=(k == 7)) if E else None
        env.tick("pe", key=("h1B", it2), inst=i)
        env.wait(eng, "vec", ("aA", it2))
        for m in (2, 3):
            for k in range(8):
                i = eng.matmul(P1[:, (m - 2) * 512:(m - 1) * 512],
                               T["wu3"][:, wp * 4096 + k * 512 + m * 128:wp * 4096 + k * 512 + (m + 1) * 128],
                               T["x2T"][:, k * TOK + ntt * 512:k * TOK + (ntt + 1) * 512],
                               start=(k == 0), stop=(k == 7)) if E else None
        env.tick("pe", key=("h3B", it2), inst=i)
        if it2 >= 1:
            _emit_w2(env, eng, T, it2 - 1)
    _emit_w2(env, eng, T, NIT2 - 1)


def _emit_w2(env, eng, T, prev):
    E = env.emit
    up = prev // 4
    wp = up % 2
    ap = prev % 2
    for blk in range(4):
        g = 4 * prev + blk
        if g >= 2:
            env.wait(eng, "vec", ("yacc", g - 2))
        pY = (T["P2"], T["P3"])[g % 2] if E else None
        for dh in range(2):
            for k in range(4):
                i = eng.matmul(pY[:, dh * 512:(dh + 1) * 512],
                               T["abf"][:, ap * 2048 + k * 512 + blk * 128:ap * 2048 + k * 512 + (blk + 1) * 128],
                               T["wu2"][:, wp * 4096 + k * 1024 + dh * 512:wp * 4096 + k * 1024 + (dh + 1) * 512],
                               start=(k == 0), stop=(k == 3)) if E else None
        env.tick("pe", key=("w2", g), inst=i)


def emit_vector(env, eng, T):
    E = env.emit
    P0, P1, P2, P3 = (T["P0"], T["P1"], T["P2"], T["P3"]) if E else (None,) * 4
    if E:
        eng.memset(T["ones_row"][0:1, :], 1.0)
        eng.memset(T["onesbf"][0:1, :], 1.0)
        eng.memset(T["onescol"][:, 0:1], 1.0)
        eng.memset(T["macc"][:, 0:1], 0.0)
        eng.memset(T["gbuf"][:, 4:8], NEG)
        eng.memset(T["epscol"][:, 0:1], EPS)
    def _Ao_stats(t):
        par3 = t % 3
        xt = T["big"][:, t * D:(t + 1) * D] if E else None
        g = t // 2
        env.wait(eng, ("xin", "xin2")[g % 2], ("Axg", g))
        if t >= 3:
            env.wait(eng, "vec", ("Avar", t - 3))
        i = None
        if E:
            eng.bn_stats(T["bnst"][:, 12 * par3:12 * par3 + 6], xt[:, 0:512])
            i = eng.bn_stats(T["bnst"][:, 12 * par3 + 6:12 * par3 + 12], xt[:, 512:1024])
        env.tick("vec", key=("Ast", t), inst=i)

    def _Ao_aggr(t):
        par = t % 4
        par3 = t % 3
        env.wait(eng, "vec", ("Ast", t))
        if t >= 4:
            env.wait(eng, "sca", ("Acast", t - 4))
        i = eng.bn_aggr(T["bnag"][:, 2 * par:2 * par + 2],
                        T["bnst"][:, 12 * par3:12 * par3 + 12]) if E else None
        env.tick("vec", key=("Avar", t), inst=i)

    def _Ao_nm(t):
        par = t % 4
        env.wait(eng, "vec", ("Avar", t))
        i = eng.tensor_scalar(T["smean"][:, par:par + 1], T["bnag"][:, 2 * par:2 * par + 1],
                              -1.0, None, op0=ALU.mult) if E else None
        env.tick("vec", key=("Anm", t), inst=i)

    def _Ao_cast(t):
        env.wait(eng, "vec", ("Anm", t))
        if t >= 10:
            env.wait(eng, "pe", ("Amm", t - 10))
        xt = T["big"][:, t * D:(t + 1) * D] if E else None
        i = eng.tensor_scalar(T["xff"][:, (t % 10) * D:(t % 10 + 1) * D], xt,
                              T["smean"][:, t % 4:t % 4 + 1], None, op0=ALU.add) if E else None
        env.tick("vec", key=("Acp", t), inst=i)

    def _Af_stats(t):
        if t % 4 == 0:
            g = (t - 16) // 4
            env.wait(eng, ("wdma", "wdma2")[g % 2], ("Afg", g))
        env.wait(eng, "vec", ("Avar", t - 3))
        par3 = t % 3
        xt = T["xff"][:, (t % 10) * D:(t % 10 + 1) * D] if E else None
        i = None
        if E:
            eng.bn_stats(T["bnst"][:, 12 * par3:12 * par3 + 6], xt[:, 0:512])
            i = eng.bn_stats(T["bnst"][:, 12 * par3 + 6:12 * par3 + 12], xt[:, 512:1024])
        env.tick("vec", key=("Ast", t), inst=i)

    def _Af_aggr(t):
        par = t % 4
        par3 = t % 3
        env.wait(eng, "vec", ("Ast", t))
        env.wait(eng, "sca", ("Acast", t - 4))
        i = eng.bn_aggr(T["bnag"][:, 2 * par:2 * par + 2],
                        T["bnst"][:, 12 * par3:12 * par3 + 12]) if E else None
        env.tick("vec", key=("Avar", t), inst=i)

    def _Af_prod(t):
        par = t % 4
        env.wait(eng, "vec", ("Avar", t))
        env.wait(eng, "sca", ("Acast", t))
        i = eng.scalar_tensor_tensor(T["macc"][:, 0:1], T["bnag"][:, 2 * par:2 * par + 1],
                                     T["rstdf"][:, par:par + 1], T["macc"][:, 0:1],
                                     op0=ALU.mult, op1=ALU.add) if E else None
        env.tick("vec", key=("Apr", t), inst=i)

    for j in range(0, 19):
        if j < 16:
            _Ao_stats(j)
        if 1 <= j <= 16:
            _Ao_aggr(j - 1)
        if 2 <= j <= 17:
            _Ao_nm(j - 2)
        if 3 <= j <= 18:
            _Ao_cast(j - 3)
    for j in range(16, NSEQ + 3):
        if j < NSEQ:
            _Af_stats(j)
        if 17 <= j <= NSEQ:
            _Af_aggr(j - 1)
        if 19 <= j <= NSEQ + 2:
            _Af_prod(j - 3)
    i = eng.tensor_copy(T["maccbf"][:, 0:1], T["macc"][:, 0:1]) if E else None
    env.tick("vec", key="maccbf", inst=i)
    env.wait(eng, "pe", ("Amm", NSEQ - 1))
    PBv = (P0, P1, P2, P3) if E else (None,) * 4
    env.wait(eng, "pe", "corr1")
    i = eng.tensor_copy(T["corrbf"][0:1, 0:1], P2[0:1, 300:301]) if E else None
    env.tick("vec", key="corrbf", inst=i)
    env.wait(eng, "pe", "corrP")
    i = eng.tensor_copy(T["corrc"][:, 0:1], P3[:, 300:301]) if E else None
    env.vbar(eng, i)
    i = None
    if E:
        for c in range(8):
            i = eng.tensor_scalar(T["scol"][:, c:c + 1],
                                  PBv[c // 2][:, (c % 2) * 512:(c % 2) * 512 + 1],
                                  T["corrc"][:, 0:1], None, op0=ALU.subtract)
    env.tick("vec", key="scol", inst=i)
    env.wait(eng, "pe", "vsum")
    i = None
    if E:
        for c in range(8):
            i = eng.tensor_copy(T["vcol"][:, c:c + 1],
                                PBv[c // 2][:, (c % 2) * 512:(c % 2) * 512 + 1])
    env.tick("vec", key="vcol", inst=i)
    env.wait(eng, "pe", "orow")
    i = eng.tensor_copy(T["orow"][0:1, :], P2[0:1, 0:1024]) if E else None
    env.vbar(eng, i)
    i = None
    if E:
        eng.bn_stats(T["bnst"][0:1, 0:6], T["orow"][0:1, 0:512])
        i = eng.bn_stats(T["bnst"][0:1, 6:12], T["orow"][0:1, 512:1024])
    env.vbar(eng, i)
    i = eng.bn_aggr(T["bnag"][0:1, 0:2], T["bnst"][0:1, 0:12]) if E else None
    env.vbar(eng, i)
    i = eng.tensor_scalar(T["smean"][0:1, 0:1], T["bnag"][0:1, 0:1],
                          -1.0, None, op0=ALU.mult) if E else None
    env.tick("vec", key="ovar", inst=i)
    if E:
        eng.wait_ge(env.sems["vec"], env.counts["vec"])
    env.wait(eng, "sca", "orstd")
    i = eng.tensor_scalar(T["orow"][0:1, :], T["orow"][0:1, :], T["smean"][0:1, 0:1],
                          T["srstd"][0:1, 0:1], op0=ALU.add, op1=ALU.mult) if E else None
    env.tick("vec", key="rrow", inst=i)
    env.wait(eng, "pe", "rbc")
    i = eng.tensor_copy(T["rbc"][:, :], P3[:, :]) if E else None
    env.tick("vec", key="rbc_sb", inst=i)

    def V1(t):
        big_t = T["big"][:, t * D:(t + 1) * D] if E else None
        i = eng.tensor_tensor(big_t, big_t, T["rbc"][:, :], op=ALU.add) if E else None
        env.tick("vec", key=("x1", t), inst=i)

    def V2(t):
        par = t % 2
        if t >= 2:
            env.wait(eng, "sca", ("B1rstd", t - 2))
        big_t = T["big"][:, t * D:(t + 1) * D] if E else None
        i = None
        if E:
            eng.bn_stats(T["bnst"][:, 0:6], big_t[:, 0:512])
            i = eng.bn_stats(T["bnst"][:, 6:12], big_t[:, 512:1024])
        env.vbar(eng, i)
        i = eng.bn_aggr(T["bnag"][:, 2 * par:2 * par + 2], T["bnst"][:, 0:12]) if E else None
        env.vbar(eng, i)
        i = eng.tensor_scalar(T["smean"][:, par:par + 1], T["bnag"][:, 2 * par:2 * par + 1],
                              -1.0, None, op0=ALU.mult) if E else None
        env.tick("vec", key=("B1var", t), inst=i)
        if E:
            eng.wait_ge(env.sems["vec"], env.counts["vec"])

    def V3(t):
        par = t % 2
        env.wait(eng, "sca", ("B1rstd", t))
        if t >= 2:
            env.wait(eng, "pe", ("tp", t - 2))
        big_t = T["big"][:, t * D:(t + 1) * D] if E else None
        i = eng.tensor_scalar(T["x2bf"][:, par * D:(par + 1) * D], big_t,
                              T["smean"][:, par:par + 1], T["srstd"][:, par:par + 1],
                              op0=ALU.add, op1=ALU.mult) if E else None
        env.tick("vec", key=("x2", t), inst=i)

    def V4(t):
        par = t % 2
        env.wait(eng, "pe", ("tp", t))
        pTPt = (P0, P1)[par] if E else None
        dst = T["x2T"][:, :].rearrange("p (c v) -> p c v", c=8)[:, :, t * 128:(t + 1) * 128] if E else None
        srcv = pTPt.rearrange("p (c v) -> p c v", c=8) if E else None
        i = eng.tensor_copy(dst, srcv) if E else None
        env.tick("vec", key=("x2Tevac", t), inst=i)

    def V5a(t):
        par = t % 2
        env.wait(eng, "pe", ("gatemm", t))
        if t >= 2:
            env.wait(eng, "sca", ("gexp", t - 2))
        pG = (P2, P3)[par] if E else None
        i = eng.reduce_max(T["m2"][:, 0:1], pG[:, 0:16], axis=AX.X) if E else None
        env.vbar(eng, i)
        i = eng.tensor_scalar(T["scr"][:, par * 16:(par + 1) * 16], pG[:, 0:16],
                              T["m2"][:, 0:1], None, op0=ALU.subtract) if E else None
        env.tick("vec", key=("gexp_in", t), inst=i)

    def V5b(t):
        env.wait(eng, "sca", ("gexp", t))
        esc = T["esc"][:, t * 16:(t + 1) * 16] if E else None
        i = eng.reduce_sum(T["m2"][:, 0:1], esc, axis=AX.X) if E else None
        env.vbar(eng, i)
        i = eng.reciprocal(T["rcp"][:, 0:1], T["m2"][:, 0:1]) if E else None
        env.vbar(eng, i)
        i = None
        if E:
            for g in range(4):
                i = eng.reduce_max(T["gbuf"][:, g:g + 1], esc[:, g * 4:(g + 1) * 4], axis=AX.X)
        env.vbar(eng, i)
        i = eng.max(out=T["gt8"][:, :], in_=T["gbuf"][:, :]) if E else None
        env.vbar(eng, i)
        i = None
        if E:
            for g in range(4):
                i = eng.tensor_scalar(T["gm"][:, g:g + 1], T["gbuf"][:, g:g + 1],
                                      T["gt8"][:, 1:2], None, op0=ALU.is_ge)
        env.vbar(eng, i)
        i = None
        if E:
            for g in range(4):
                i = eng.tensor_scalar(T["masked"][:, g * 4:(g + 1) * 4], esc[:, g * 4:(g + 1) * 4],
                                      T["gm"][:, g:g + 1], None, op0=ALU.mult)
        env.vbar(eng, i)
        i = eng.max(out=T["gt8"][:, :], in_=T["masked"][:, :]) if E else None
        env.vbar(eng, i)
        i = eng.tensor_scalar(T["maske"][:, :], T["masked"][:, :],
                              T["gt8"][:, 3:4], None, op0=ALU.is_ge) if E else None
        env.vbar(eng, i)
        i = eng.tensor_mul(T["masked"][:, :], esc, T["maske"][:, :]) if E else None
        env.vbar(eng, i)
        i = eng.tensor_scalar(T["c_all"][:, t * 16:(t + 1) * 16], T["masked"][:, :],
                              T["rcp"][:, 0:1], None, op0=ALU.mult) if E else None
        env.tick("vec", key=("gate", t), inst=i)

    stages = (V5a, V4, V3, V2, V1)
    for s in range(NTO + 4):
        for j, fn in enumerate(stages):
            t = s - (4 - j)
            if 0 <= t < NTO:
                fn(t)

    def F1(t):
        par = t % 2
        env.wait(eng, "vec", ("yacc", 4 * (17 * 4 + t // 4) + (t % 4)))
        if t >= 2:
            env.wait(eng, "sca", ("Frstd", t - 2))
        yv = T["big"][:, t * D:(t + 1) * D] if E else None
        i = None
        if E:
            eng.bn_stats(T["bnst"][:, 0:6], yv[:, 0:512])
            i = eng.bn_stats(T["bnst"][:, 6:12], yv[:, 512:1024])
        env.vbar(eng, i)
        i = eng.bn_aggr(T["bnag"][:, 2 * par:2 * par + 2], T["bnst"][:, 0:12]) if E else None
        env.vbar(eng, i)
        i = eng.tensor_scalar(T["smean"][:, par:par + 1], T["bnag"][:, 2 * par:2 * par + 1],
                              -1.0, None, op0=ALU.mult) if E else None
        env.tick("vec", key=("Fvar", t), inst=i)
        if E:
            eng.wait_ge(env.sems["vec"], env.counts["vec"])

    def F2(t):
        par = t % 2
        env.wait(eng, "sca", ("Frstd", t))
        if t >= 2:
            env.wait(eng, ("xin", "xin2")[t % 2], ("out", t - 2))
        env.wait(eng, ("wdma", "wdma2")[(t // 4) % 2], ("rdg", t // 4))
        yv = T["big"][:, t * D:(t + 1) * D] if E else None
        ost = T["outst"][:, par * D:(par + 1) * D] if E else None
        i = None
        if E:
            eng.tensor_scalar(ost, yv, T["smean"][:, par:par + 1],
                              T["srstd"][:, par:par + 1], op0=ALU.add, op1=ALU.mult)
            i = eng.tensor_add(ost, ost, T["x1rd"][:, (t % 4) * D:(t % 4) * D + D])
        env.tick("vec", key=("fin", t), inst=i)




    def _yacc(prev):
        up = prev // 4
        for blk in range(4):
            g = 4 * prev + blk
            tj = (prev % 4) * 4 + blk
            env.wait(eng, "pe", ("w2", g))
            pY = (P2, P3)[g % 2] if E else None
            yv = T["big"][:, tj * D:(tj + 1) * D] if E else None
            i = None
            if up == 0:
                env.wait(eng, "spill", ("spg", tj // 4))
                i = eng.tensor_copy(yv, pY[:, :]) if E else None
            elif up == 1:
                i = eng.tensor_add(yv, yv, pY[:, :]) if E else None
            else:
                ccol = T["c_all"][:, tj * 16 + (up - 2):tj * 16 + (up - 2) + 1] if E else None
                i = eng.scalar_tensor_tensor(yv, pY[:, :], ccol, yv,
                                             op0=ALU.mult, op1=ALU.add) if E else None
            env.tick("vec", key=("yacc", g), inst=i)

    fin_steps = []

    def _fin_step(s):
        if 0 <= s - 1 < NTO:
            F2(s - 1)
        if s < NTO:
            F1(s)

    for it2 in range(NIT2):
        ap = it2 % 2
        env.wait(eng, "pe", ("h3A", it2))
        env.wait(eng, "sca", ("siluA", it2))
        if it2 >= 2:
            env.wait(eng, "pe", ("w2", 4 * (it2 - 2) + 3))
        i = eng.tensor_tensor(T["abf"][:, ap * 2048:ap * 2048 + 1024],
                              T["s1"][:, 0:1024], P1[:, :], op=ALU.mult) if E else None
        env.tick("vec", key=("aA", it2), inst=i)
        env.wait(eng, "pe", ("h3B", it2))
        env.wait(eng, "sca", ("siluB", it2))
        i = eng.tensor_tensor(T["abf"][:, ap * 2048 + 1024:ap * 2048 + 2048],
                              T["s1"][:, 1024:2048], P1[:, :], op=ALU.mult) if E else None
        env.tick("vec", key=("aB", it2), inst=i)
        if it2 >= 1:
            _yacc(it2 - 1)
            prev = it2 - 1
            if prev >= 68:
                j = prev - 68
                for q in range(4):
                    _fin_step(4 * j + q)
        if it2 < 8:
            V5b(2 * it2)
            V5b(2 * it2 + 1)
    _yacc(NIT2 - 1)
    for s in range(12, NTO + 1):
        _fin_step(s)

def _xtile(T, t):
    if t < 16:
        return T["big"][:, t * D:(t + 1) * D]
    fi = t - 16
    return T["xff"][:, (fi % 8) * D:(fi % 8) * D + D]


def emit_scalar(env, eng, T):
    E = env.emit
    P0 = T["P0"] if E else None

    def SA2(t):
        par = t % 4
        env.wait(eng, "vec", ("Avar", t))
        if t >= 4:
            env.wait(eng, "pe", ("Amm", t - 4))
        i = None
        if E:
            eng.activation(T["lncol"][:, 0:1], T["bnag"][:, 2 * par + 1:2 * par + 2],
                           AF.Ln, bias=T["epscol"][:, 0:1])
            i = eng.activation(T["rcol"][:, par:par + 1], T["lncol"][:, 0:1], AF.Exp, scale=-0.5)
            if t >= 16:
                i = eng.activation(T["rstdf"][:, par:par + 1], T["lncol"][:, 0:1],
                                   AF.Exp, scale=-0.5)
        env.tick("sca", key=("Acast", t), inst=i)

    for s in range(NSEQ):
        SA2(s)
    env.wait(eng, "vec", "ovar")
    i = None
    if E:
        eng.activation(T["lncol"][0:1, 0:1], T["bnag"][0:1, 1:2], AF.Ln,
                       bias=T["epscol"][0:1, 0:1])
        i = eng.activation(T["srstd"][0:1, 0:1], T["lncol"][0:1, 0:1], AF.Exp, scale=-0.5)
    env.tick("sca", key="orstd", inst=i)

    def S2(t):
        par = t % 2
        env.wait(eng, "vec", ("B1var", t))
        i = None
        if E:
            eng.activation(T["lncol"][:, 0:1], T["bnag"][:, 2 * par + 1:2 * par + 2],
                           AF.Ln, bias=T["epscol"][:, 0:1])
            i = eng.activation(T["srstd"][:, par:par + 1], T["lncol"][:, 0:1], AF.Exp, scale=-0.5)
        env.tick("sca", key=("B1rstd", t), inst=i)

    def S3(t):
        par = t % 2
        env.wait(eng, "vec", ("gexp_in", t))
        i = eng.activation(T["esc"][:, t * 16:(t + 1) * 16],
                           T["scr"][:, par * 16:(par + 1) * 16], AF.Exp) if E else None
        env.tick("sca", key=("gexp", t), inst=i)

    for s in range(NTO + 4):
        for fn, lag in ((S3, 4), (S2, 0)):
            t = s - lag
            if 0 <= t < NTO:
                fn(t)

    def FS2(t):
        par = t % 2
        env.wait(eng, "vec", ("Fvar", t))
        i = None
        if E:
            eng.activation(T["lncol"][:, 0:1], T["bnag"][:, 2 * par + 1:2 * par + 2],
                           AF.Ln, bias=T["epscol"][:, 0:1])
            i = eng.activation(T["srstd"][:, par:par + 1], T["lncol"][:, 0:1], AF.Exp, scale=-0.5)
        env.tick("sca", key=("Frstd", t), inst=i)




    def _fs_step(s):
        if 0 <= s < NTO:
            FS2(s)

    for it2 in range(NIT2):
        env.wait(eng, "pe", ("h1A", it2))
        if it2 >= 1:
            env.wait(eng, "vec", ("aA", it2 - 1))
        i = eng.activation(T["s1"][:, 0:1024], P0[:, :], AF.Silu) if E else None
        env.tick("sca", key=("siluA", it2), inst=i)
        env.wait(eng, "pe", ("h1B", it2))
        if it2 >= 1:
            env.wait(eng, "vec", ("aB", it2 - 1))
        i = eng.activation(T["s1"][:, 1024:2048], P0[:, :], AF.Silu) if E else None
        env.tick("sca", key=("siluB", it2), inst=i)
        if it2 >= 69:
            j = it2 - 69
            for q in range(4):
                _fs_step(4 * j + q)
    for s in range(12, NTO):
        _fs_step(s)

def build_program():
    nc = bass.Bass("TRN2", target_bir_lowering=False, dynamic_dma_scratch_size=16384,
                   detect_race_conditions=False)
    T = {}
    T["x"] = nc.dram_tensor("x", [8192, D], F32, kind="ExternalInput")
    T["wvT"] = nc.dram_tensor("wvT", [D, D], F32, kind="ExternalInput")
    T["woT"] = nc.dram_tensor("woT", [D, D], F32, kind="ExternalInput")
    T["gate_wT"] = nc.dram_tensor("gate_wT", [D, 16], F32, kind="ExternalInput")
    T["w1"] = nc.dram_tensor("w1", [16, D, 512], F32, kind="ExternalInput")
    T["w2"] = nc.dram_tensor("w2", [16, 512, D], F32, kind="ExternalInput")
    T["w3"] = nc.dram_tensor("w3", [16, D, 512], F32, kind="ExternalInput")
    T["ws1"] = nc.dram_tensor("ws1", [D, D], F32, kind="ExternalInput")
    T["ws2"] = nc.dram_tensor("ws2", [D, D], F32, kind="ExternalInput")
    T["ws3"] = nc.dram_tensor("ws3", [D, D], F32, kind="ExternalInput")
    T["out"] = nc.dram_tensor("out", [TOK, D], F32, kind="ExternalOutput")
    T["x1spill"] = nc.dram_tensor("x1spill", [TOK, D], F32,
                                  kind="ExternalOutput" if DEBUG else "Internal")
    if DEBUG:
        T["dbg_y"] = nc.dram_tensor("dbg_y", [128, NTO * D], F32, kind="ExternalOutput")
        T["dbg_c"] = nc.dram_tensor("dbg_c", [128, NTO * 16], F32, kind="ExternalOutput")
        T["dbg_rbc"] = nc.dram_tensor("dbg_rbc", [128, D], F32, kind="ExternalOutput")
        T["dbg_scol"] = nc.dram_tensor("dbg_scol", [128, 8], BF16, kind="ExternalOutput")
        T["dbg_x2T"] = nc.dram_tensor("dbg_x2T", [128, 8 * TOK], BF16, kind="ExternalOutput")
        T["dbg_gwt"] = nc.dram_tensor("dbg_gwt", [128, 8 * 16], BF16, kind="ExternalOutput")
        T["dbg_orow"] = nc.dram_tensor("dbg_orow", [1, D], F32, kind="ExternalOutput")
        T["dbg_vcol"] = nc.dram_tensor("dbg_vcol", [128, 8], BF16, kind="ExternalOutput")
        T["dbg_esc"] = nc.dram_tensor("dbg_esc", [128, 32], F32, kind="ExternalOutput")
        T["dbg_scores"] = nc.dram_tensor("dbg_scores", [128, 16], F32, kind="ExternalOutput")

    with ExitStack() as ctx:
        e = ctx.enter_context

        def sb(name, shape, dt):
            h = e(nc.sbuf_tensor(name, shape, dt))
            T[name] = h
            return h

        sb("big", [128, NTO * D], F32)
        sb("x2T", [128, 8 * TOK], BF16)
        sb("wu1", [128, 2 * 4096], BF16)
        sb("wu3", [128, 2 * 4096], BF16)
        sb("wu2", [128, 2 * 4096], BF16)
        sb("gwt", [128, 8 * 16], BF16)
        sb("xff", [128, 10 * D], BF16)
        sb("x1rd", [128, 4 * D], BF16)
        sb("s1", [128, 2 * D], BF16)
        sb("abf", [128, 2 * 2048], BF16)
        sb("c_all", [128, NTO * 16], F32)
        sb("x2bf", [128, 2 * D], BF16)
        sb("ident", [128, 128], BF16)
        sb("outst", [128, 2 * D], F32)
        sb("rcp", [128, 1], F32)
        sb("bnst", [128, 36], F32)
        sb("bnag", [128, 8], F32)
        sb("rstdf", [128, 4], F32)
        sb("prodc", [128, 1], F32)
        sb("macc", [128, 1], F32)
        sb("maccbf", [128, 1], BF16)
        sb("corrc", [128, 1], F32)
        sb("corrbf", [1, 1], BF16)
        sb("onesbf", [1, 128], BF16)
        sb("onescol", [128, 1], BF16)
        sb("epscol", [128, 1], F32)
        sb("rbc", [128, D], BF16)
        sb("smean", [128, 4], F32)
        sb("srstd", [128, 4], F32)
        sb("m2", [128, 1], F32)
        sb("lncol", [128, 1], F32)
        sb("rcol", [128, 4], BF16)
        sb("scol", [128, 8], BF16)
        sb("vcol", [128, 8], BF16)
        sb("orow", [1, D], F32)
        sb("ones_row", [1, 128], F32)
        sb("scr", [128, 2 * 16], F32)
        sb("esc", [128, 16 * 16], F32)
        sb("gbuf", [128, 8], F32)
        sb("gt8", [128, 8], F32)
        sb("gm", [128, 4], F32)
        sb("masked", [128, 16], F32)
        sb("maske", [128, 16], F32)

        T["P0"] = e(nc.psum_tensor("P0", [128, 1024], F32))
        T["P1"] = e(nc.psum_tensor("P1", [128, 1024], F32))
        T["P2"] = e(nc.psum_tensor("P2", [128, 1024], F32))
        T["P3"] = e(nc.psum_tensor("P3", [128, 1024], F32))

        env = Env()
        emit_sync(env, None, T)
        emit_gpsimd(env, None, T)
        emit_tensor(env, None, T)
        emit_vector(env, None, T)
        emit_scalar(env, None, T)
        env.check_pass1()
        pass1_counts = dict(env.counts)

        env.emit = True
        env.reset()
        for s in SEMS:
            env.sems[s] = e(nc.semaphore(s))
        block = e(nc.Block())
        block.sync(lambda eng: emit_sync(env, eng, T))
        block.gpsimd(lambda eng: emit_gpsimd(env, eng, T))
        block.tensor(lambda eng: emit_tensor(env, eng, T))
        block.vector(lambda eng: emit_vector(env, eng, T))
        block.scalar(lambda eng: emit_scalar(env, eng, T))
        assert env.counts == pass1_counts, (env.counts, pass1_counts)
    return nc


_NC_CACHE = []


def kernel(L0_attn_x, L0_attn_w_qkv, L0_attn_w_o, L0_ffn_gate_w, L0_ffn_w1,
           L0_ffn_w2, L0_ffn_w3, L0_ffn_ws1, L0_ffn_ws2, L0_ffn_ws3):
    x = np.ascontiguousarray(np.asarray(L0_attn_x, dtype=np.float32))
    wvT = np.ascontiguousarray(np.asarray(L0_attn_w_qkv, dtype=np.float32)[2048:3072, :].T)
    woT = np.ascontiguousarray(np.asarray(L0_attn_w_o, dtype=np.float32).T)
    gate_wT = np.ascontiguousarray(np.asarray(L0_ffn_gate_w, dtype=np.float32).T)
    w1 = np.ascontiguousarray(np.asarray(L0_ffn_w1, dtype=np.float32))
    w2 = np.ascontiguousarray(np.asarray(L0_ffn_w2, dtype=np.float32))
    w3 = np.ascontiguousarray(np.asarray(L0_ffn_w3, dtype=np.float32))
    ws1 = np.ascontiguousarray(np.asarray(L0_ffn_ws1, dtype=np.float32))
    ws2 = np.ascontiguousarray(np.asarray(L0_ffn_ws2, dtype=np.float32))
    ws3 = np.ascontiguousarray(np.asarray(L0_ffn_ws3, dtype=np.float32))

    if not _NC_CACHE:
        _NC_CACHE.append(build_program())
    nc = _NC_CACHE[0]

    in_maps = []
    for core in range(8):
        b, q = divmod(core, 4)
        xb = x[b]
        x_seq = np.concatenate(
            [xb[q * TOK:(q + 1) * TOK], xb[:q * TOK], xb[(q + 1) * TOK:]], axis=0)
        in_maps.append({
            "x": np.ascontiguousarray(x_seq),
            "wvT": wvT, "woT": woT, "gate_wT": gate_wT,
            "w1": w1, "w2": w2, "w3": w3,
            "ws1": ws1, "ws2": ws2, "ws3": ws3,
        })
    res = run_bass_kernel_spmd(nc, in_maps, core_ids=list(range(8)))
    out = np.empty((2, 8192, D), dtype=np.float32)
    for core in range(8):
        b, q = divmod(core, 4)
        out[b, q * TOK:(q + 1) * TOK] = np.asarray(res.results[core]["out"])
    return out


# version 40 (final)
# speedup vs baseline: 1.0116x; 1.0041x over previous
import numpy as np
from contextlib import ExitStack

import concourse.bass as bass
import concourse.mybir as mybir
from concourse.bass_utils import run_bass_kernel_spmd

F32 = mybir.dt.float32
BF16 = mybir.dt.bfloat16
AF = mybir.ActivationFunctionType
ALU = mybir.AluOpType
AX = mybir.AxisListType

D = 1024
TOK = 2048
NTO = 16
NSEQ = 64
NUNITS = 18
NIT = NUNITS * 8
NIT2 = NUNITS * 4
EPS = 1e-5
NEG = -1.0e30

SEMS = ("xin", "xin2", "spill", "wdma", "wdma2", "gp", "pe", "vec", "sca")
DEBUG = False


class Env:
    def __init__(self):
        self.emit = False
        self.counts = {s: 0 for s in SEMS}
        self.ckpts = {}
        self.reqs = set()
        self.sems = {}

    def reset(self):
        self.counts = {s: 0 for s in SEMS}

    def tick(self, sem, key=None, n=1, inst=None):
        c = self.counts[sem] + n
        self.counts[sem] = c
        if key is not None:
            k = (sem, key)
            if self.emit:
                assert self.ckpts[k] == c, (k, self.ckpts[k], c)
            else:
                assert k not in self.ckpts, k
                self.ckpts[k] = c
        if self.emit and inst is not None:
            inst.then_inc(self.sems[sem], n)
        return c

    def at(self, sem, key):
        k = (sem, key)
        if not self.emit:
            self.reqs.add(k)
            return self.ckpts.get(k, 0)
        return self.ckpts[k]

    def wait(self, eng, sem, key):
        v = self.at(sem, key)
        if self.emit and v > 0:
            eng.wait_ge(self.sems[sem], v)

    def vbar(self, eng, inst):
        c = self.tick("vec", inst=inst)
        if self.emit:
            eng.wait_ge(self.sems["vec"], c)

    def check_pass1(self):
        missing = [k for k in self.reqs if k not in self.ckpts]
        assert not missing, f"unresolved ckpts: {sorted(missing)[:10]}"


def _unit_srcs(T, u):
    if u < 2:
        lo, hi = (0, 512) if u == 0 else (512, 1024)
        w1s = T["ws1"][:, :].rearrange("(k p) i -> p k i", p=128)[:, :, lo:hi]
        w3s = T["ws3"][:, :].rearrange("(k p) i -> p k i", p=128)[:, :, lo:hi]
        w2s = T["ws2"][lo:hi, :].rearrange("(k p) d -> p k d", p=128)
    else:
        e = u - 2
        w1s = T["w1"][e].rearrange("(k p) i -> p k i", p=128)
        w3s = T["w3"][e].rearrange("(k p) i -> p k i", p=128)
        w2s = T["w2"][e].rearrange("(k p) d -> p k d", p=128)
    return w1s, w3s, w2s


def emit_sync(env, eng, T):
    E = env.emit
    for t in range(16):
        g = t // 2
        sem = ("xin", "xin2")[g % 2]
        if t % 2 == 0 and g >= 2:
            if E:
                eng.wait_ge(env.sems[sem], env.at(sem, ("Axg", g - 2)))
        dst = T["big"][:, t * D:(t + 1) * D] if E else None
        i = eng.dma_start(out=dst, in_=T["x"][t * 128:(t + 1) * 128, :]) if E else None
        env.tick(sem, n=16, inst=i)
        if t % 2 == 1:
            env.tick(sem, key=("Axg", g), n=0)
    for t in range(NTO):
        if t % 4 == 0 and t > 0 and E:
            eng.wait_ge(env.sems["spill"], env.counts["spill"])
        env.wait(eng, "vec", ("x1", t))
        i = eng.dma_start(out=T["x1spill"][t * 128:(t + 1) * 128, :],
                          in_=T["big"][:, t * D:(t + 1) * D]) if E else None
        env.tick("spill", n=16, inst=i)
        if t % 4 == 3:
            env.tick("spill", key=("spg", t // 4), n=0)
    if DEBUG:
        env.wait(eng, "vec", ("fin", NTO - 1))
        for nm, src in (("dbg_y", T["big"][:, :]), ("dbg_c", T["c_all"][:, :]),
                        ("dbg_rbc", T["rbc"][:, :]), ("dbg_scol", T["scol"][:, :]),
                        ("dbg_x2T", T["x2T"][:, :]), ("dbg_gwt", T["gwt"][:, :]),
                        ("dbg_orow", T["orow"][0:1, :]), ("dbg_vcol", T["vcol"][:, :]),
                        ("dbg_esc", T["esc"][:, :]), ("dbg_scores", T["scores"][:, :])):
            i = eng.dma_start(out=T[nm][:, :], in_=src) if E else None
            env.tick("spill", n=16, inst=i)
    for t in range(NTO):
        osem = ("xin", "xin2")[t % 2]
        if t >= 2 and E:
            eng.wait_ge(env.sems[osem], env.at(osem, ("out", t - 2)))
        env.wait(eng, "vec", ("fin", t))
        i = eng.dma_start(out=T["out"][t * 128:(t + 1) * 128, :],
                          in_=T["outst"][:, (t % 2) * D:(t % 2) * D + D]) if E else None
        env.tick(osem, key=("out", t), n=16, inst=i)
    if E:
        eng.wait_ge(env.sems["spill"], env.counts["spill"])
        eng.wait_ge(env.sems["wdma"], env.counts["wdma"])
        eng.wait_ge(env.sems["wdma2"], env.counts["wdma2"])
        eng.wait_ge(env.sems["xin"], env.counts["xin"])
        eng.wait_ge(env.sems["xin2"], env.counts["xin2"])


def emit_gpsimd(env, eng, T):
    E = env.emit
    if E:
        eng.memset(T["ident"][:, :], 0.0)
    i = eng.affine_select(out=T["ident"][:, :], in_=T["ident"][:, :],
                          compare_op=ALU.not_equal, fill=1.0, base=0,
                          pattern=[[-1, 128]], channel_multiplier=1) if E else None
    env.tick("gp", key="ident", inst=i)
    def _fx2(t):
        fi = t - 16
        g = fi // 4
        sem = ("wdma", "wdma2")[g % 2]
        if fi % 4 == 0 and g >= 4:
            if E:
                eng.wait_ge(env.sems[sem], env.at(sem, ("Afg", g - 4)))
        prev = t - 9
        env.wait(eng, "vec", ("Acp", prev) if prev < 16 else ("Ast", prev))
        env.wait(eng, "pe", ("Amm", prev))
        for u in (t, t + 1):
            i = eng.dma_start(out=T["xff"][:, (u % 10) * D:(u % 10) * D + D],
                              in_=T["x"][u * 128:(u + 1) * 128, :]) if E else None
            env.tick(sem, n=16, inst=i)
        if fi % 4 == 2:
            env.tick(sem, key=("Afg", g), n=0)

    i = eng.dma_start(out=T["gwt"][:, :].rearrange("p (c e) -> p c e", c=8),
                      in_=T["gate_wT"][:, :].rearrange("(c p) e -> p c e", p=128)) if E else None
    env.tick("wdma", key=("w", "gate"), n=16, inst=i)
    i = eng.dma_start(out=T["wu1"][:, :].rearrange("p (c j) -> p c j", c=8),
                      in_=T["wvT"][:, :].rearrange("(c p) j -> p c j", p=128)) if E else None
    env.tick("wdma", key=("w", "wv"), n=16, inst=i)
    i = eng.dma_start(out=T["wu3"][:, :].rearrange("p (c j) -> p c j", c=8),
                      in_=T["woT"][:, :].rearrange("(c p) j -> p c j", p=128)) if E else None
    env.tick("wdma", key=("w", "wo"), n=16, inst=i)
    for t in range(16, NSEQ, 2):
        _fx2(t)
    if E:
        eng.wait_ge(env.sems["wdma"], env.counts["wdma"])
        eng.wait_ge(env.sems["wdma2"], env.counts["wdma2"])
    for u in range(NUNITS):
        p = u % 2
        if E:
            eng.wait_ge(env.sems["wdma"], env.counts["wdma"])
        if u == 0 or u == 1:
            env.wait(eng, "pe", "vsum")
        else:
            env.wait(eng, "pe", ("w2", 4 * ((u - 2) * 4 + 3) + 3))
        w1s, w3s, w2s = _unit_srcs(T, u) if E else (None, None, None)
        i = eng.dma_start(out=T["wu1"][:, p * 4096:(p + 1) * 4096].rearrange("p (k i) -> p k i", k=8),
                          in_=w1s) if E else None
        env.tick("wdma", n=16, inst=i)
        if u == 0 or u == 1:
            env.wait(eng, "pe", "orow")
        i = eng.dma_start(out=T["wu3"][:, p * 4096:(p + 1) * 4096].rearrange("p (k i) -> p k i", k=8),
                          in_=w3s) if E else None
        env.tick("wdma", n=16, inst=i)
        i = eng.dma_start(out=T["wu2"][:, p * 4096:(p + 1) * 4096].rearrange("p (k d) -> p k d", k=4),
                          in_=w2s) if E else None
        env.tick("wdma", key=("w", ("u", u)), n=16, inst=i)
    for t in range(NTO):
        g = t // 4
        rsem = ("wdma", "wdma2")[g % 2]
        if t % 4 == 0 and g >= 2 and E:
            eng.wait_ge(env.sems[rsem], env.at(rsem, ("rdg", g - 2)))
        if t >= 4:
            env.wait(eng, "vec", ("fin", t - 4))
        i = eng.dma_start(out=T["x1rd"][:, (t % 4) * D:(t % 4) * D + D],
                          in_=T["x1spill"][t * 128:(t + 1) * 128, :]) if E else None
        env.tick(rsem, n=16, inst=i)
        if t % 4 == 3:
            env.tick(rsem, key=("rdg", g), n=0)


def emit_tensor(env, eng, T):
    E = env.emit
    P0, P1, P2, P3 = (T["P0"], T["P1"], T["P2"], T["P3"]) if E else (None,) * 4
    PB = (P0, P1, P2, P3) if E else (None,) * 4
    for t in range(NSEQ):
        env.wait(eng, "sca", ("Acast", t))
        if t < 16:
            env.wait(eng, "vec", ("Acp", t))
        elif t % 4 == 0:
            env.wait(eng, ("wdma", "wdma2")[((t - 16) // 4) % 2], ("Afg", (t - 16) // 4))
        for c in range(8):
            dst = PB[c // 2][:, (c % 2) * 512:(c % 2) * 512 + 1] if E else None
            i = eng.matmul(dst,
                           T["xff"][:, (t % 10) * D + c * 128:(t % 10) * D + (c + 1) * 128],
                           T["rcol"][:, t % 4:t % 4 + 1],
                           start=(t == 0), stop=(t == NSEQ - 1)) if E else None
        env.tick("pe", key=("Amm", t), inst=i)
    env.wait(eng, "vec", "maccbf")
    i = eng.matmul(P2[0:1, 300:301], T["maccbf"][:, 0:1], T["onescol"][:, 0:1],
                   start=True, stop=True) if E else None
    env.tick("pe", key="corr1", inst=i)
    env.wait(eng, "vec", "corrbf")
    i = eng.matmul(P3[:, 300:301], T["onesbf"][0:1, 0:128], T["corrbf"][0:1, 0:1],
                   start=True, stop=True) if E else None
    env.tick("pe", key="corrP", inst=i)
    env.wait(eng, "vec", "scol")
    env.wait(eng, "wdma", ("w", "wo"))
    for jc in range(8):
        for dc in range(8):
            dst = PB[jc // 2][:, (jc % 2) * 512:(jc % 2) * 512 + 1] if E else None
            i = eng.matmul(dst,
                           T["wu1"][:, dc * 1024 + jc * 128:dc * 1024 + (jc + 1) * 128],
                           T["scol"][:, dc:dc + 1],
                           start=(dc == 0), stop=(dc == 7)) if E else None
    env.tick("pe", key="vsum", inst=i)
    env.wait(eng, "vec", "vcol")
    env.wait(eng, "wdma", ("w", "wo"))
    for c in range(8):
        for h in range(2):
            i = eng.matmul(P2[0:1, h * 512:(h + 1) * 512],
                           T["vcol"][:, c:c + 1],
                           T["wu3"][:, c * 1024 + h * 512:c * 1024 + (h + 1) * 512],
                           start=(c == 0), stop=(c == 7)) if E else None
    env.tick("pe", key="orow", inst=i)
    env.wait(eng, "vec", "rrow")
    i = eng.matmul(P3[:, 0:512], T["ones_row"][0:1, :], T["orow"][0:1, 0:512],
                   start=True, stop=True) if E else None
    i = eng.matmul(P3[:, 512:1024], T["ones_row"][0:1, :], T["orow"][0:1, 512:1024],
                   start=True, stop=True) if E else None
    env.tick("pe", key="rbc", inst=i)
    pTP = (P0, P1) if E else (None, None)
    pG = (P2, P3) if E else (None, None)
    env.wait(eng, "gp", "ident")
    env.wait(eng, "wdma", ("w", "gate"))
    for t in range(NTO):
        par = t % 2
        env.wait(eng, "vec", ("x2", t))
        if t >= 2:
            env.wait(eng, "vec", ("x2Tevac", t - 2))
        for c in range(8):
            i = eng.matmul(pTP[par][:, c * 128:(c + 1) * 128],
                           T["x2bf"][:, par * D + c * 128:par * D + (c + 1) * 128],
                           T["ident"][:, :], start=True, stop=True) if E else None
        env.tick("pe", key=("tp", t), inst=i)
        env.wait(eng, "vec", ("x2Tevac", t))
        if t >= 2:
            env.wait(eng, "sca", ("glog", t - 2))
        for c in range(8):
            i = eng.matmul(pG[par][:, 0:16],
                           T["x2T"][:, c * TOK + t * 128:c * TOK + (t + 1) * 128],
                           T["gwt"][:, c * 16:(c + 1) * 16],
                           start=(c == 0), stop=(c == 7)) if E else None
        env.tick("pe", key=("gatemm", t), inst=i)
    for it2 in range(NIT2):
        u, ntt = divmod(it2, 4)
        wp = u % 2
        if ntt == 0:
            env.wait(eng, "wdma", ("w", ("u", u)))
        if it2 == 0:
            env.wait(eng, "vec", ("x2Tevac", NTO - 1))
            env.wait(eng, "sca", ("glog", NTO - 1))
        if it2 >= 1:
            env.wait(eng, "sca", ("siluB", it2 - 1))
        for m in (0, 1):
            for k in range(8):
                i = eng.matmul(P0[:, m * 512:(m + 1) * 512],
                               T["wu1"][:, wp * 4096 + k * 512 + m * 128:wp * 4096 + k * 512 + (m + 1) * 128],
                               T["x2T"][:, k * TOK + ntt * 512:k * TOK + (ntt + 1) * 512],
                               start=(k == 0), stop=(k == 7)) if E else None
        env.tick("pe", key=("h1A", it2), inst=i)
        if it2 >= 1:
            env.wait(eng, "vec", ("aB", it2 - 1))
        for m in (0, 1):
            for k in range(8):
                i = eng.matmul(P1[:, m * 512:(m + 1) * 512],
                               T["wu3"][:, wp * 4096 + k * 512 + m * 128:wp * 4096 + k * 512 + (m + 1) * 128],
                               T["x2T"][:, k * TOK + ntt * 512:k * TOK + (ntt + 1) * 512],
                               start=(k == 0), stop=(k == 7)) if E else None
        env.tick("pe", key=("h3A", it2), inst=i)
        env.wait(eng, "sca", ("siluA", it2))
        for m in (2, 3):
            for k in range(8):
                i = eng.matmul(P0[:, (m - 2) * 512:(m - 1) * 512],
                               T["wu1"][:, wp * 4096 + k * 512 + m * 128:wp * 4096 + k * 512 + (m + 1) * 128],
                               T["x2T"][:, k * TOK + ntt * 512:k * TOK + (ntt + 1) * 512],
                               start=(k == 0), stop=(k == 7)) if E else None
        env.tick("pe", key=("h1B", it2), inst=i)
        env.wait(eng, "vec", ("aA", it2))
        for m in (2, 3):
            for k in range(8):
                i = eng.matmul(P1[:, (m - 2) * 512:(m - 1) * 512],
                               T["wu3"][:, wp * 4096 + k * 512 + m * 128:wp * 4096 + k * 512 + (m + 1) * 128],
                               T["x2T"][:, k * TOK + ntt * 512:k * TOK + (ntt + 1) * 512],
                               start=(k == 0), stop=(k == 7)) if E else None
        env.tick("pe", key=("h3B", it2), inst=i)
        if it2 >= 1:
            _emit_w2(env, eng, T, it2 - 1)
    _emit_w2(env, eng, T, NIT2 - 1)


def _emit_w2(env, eng, T, prev):
    E = env.emit
    up = prev // 4
    wp = up % 2
    ap = prev % 2
    for blk in range(4):
        g = 4 * prev + blk
        if g >= 2:
            env.wait(eng, "vec", ("yacc", g - 2))
        pY = (T["P2"], T["P3"])[g % 2] if E else None
        for dh in range(2):
            for k in range(4):
                i = eng.matmul(pY[:, dh * 512:(dh + 1) * 512],
                               T["abf"][:, ap * 2048 + k * 512 + blk * 128:ap * 2048 + k * 512 + (blk + 1) * 128],
                               T["wu2"][:, wp * 4096 + k * 1024 + dh * 512:wp * 4096 + k * 1024 + (dh + 1) * 512],
                               start=(k == 0), stop=(k == 3)) if E else None
        env.tick("pe", key=("w2", g), inst=i)


def emit_vector(env, eng, T):
    E = env.emit
    P0, P1, P2, P3 = (T["P0"], T["P1"], T["P2"], T["P3"]) if E else (None,) * 4
    if E:
        eng.memset(T["ones_row"][0:1, :], 1.0)
        eng.memset(T["onesbf"][0:1, :], 1.0)
        eng.memset(T["onescol"][:, 0:1], 1.0)
        eng.memset(T["macc"][:, 0:1], 0.0)
        eng.memset(T["gbuf"][:, 4:8], NEG)
        eng.memset(T["epscol"][:, 0:1], EPS)
    def _Ao_v(t):
        par = t % 4
        xt = T["big"][:, t * D:(t + 1) * D] if E else None
        g = t // 2
        env.wait(eng, ("xin", "xin2")[g % 2], ("Axg", g))
        if t >= 4:
            env.wait(eng, "sca", ("Acast", t - 4))
        i = None
        if E:
            eng.bn_stats(T["bnst"][:, 0:6], xt[:, 0:512])
            i = eng.bn_stats(T["bnst"][:, 6:12], xt[:, 512:1024])
        env.vbar(eng, i)
        i = eng.bn_aggr(T["bnag"][:, 2 * par:2 * par + 2], T["bnst"][:, 0:12]) if E else None
        env.vbar(eng, i)
        i = eng.tensor_scalar(T["smean"][:, par:par + 1], T["bnag"][:, 2 * par:2 * par + 1],
                              -1.0, None, op0=ALU.mult) if E else None
        env.tick("vec", key=("Avar", t), inst=i)
        if E:
            eng.wait_ge(env.sems["vec"], env.counts["vec"])
        if t >= 10:
            env.wait(eng, "pe", ("Amm", t - 10))
        i = eng.tensor_scalar(T["xff"][:, (t % 10) * D:(t % 10 + 1) * D], xt,
                              T["smean"][:, par:par + 1], None, op0=ALU.add) if E else None
        env.tick("vec", key=("Acp", t), inst=i)

    def _Af_stats(t):
        if t % 4 == 0:
            g = (t - 16) // 4
            env.wait(eng, ("wdma", "wdma2")[g % 2], ("Afg", g))
        if t >= 19:
            env.wait(eng, "vec", ("Avar", t - 3))
        par3 = t % 3
        xt = T["xff"][:, (t % 10) * D:(t % 10 + 1) * D] if E else None
        i = None
        if E:
            eng.bn_stats(T["bnst"][:, 12 * par3:12 * par3 + 6], xt[:, 0:512])
            i = eng.bn_stats(T["bnst"][:, 12 * par3 + 6:12 * par3 + 12], xt[:, 512:1024])
        env.tick("vec", key=("Ast", t), inst=i)

    def _Af_aggr(t):
        par = t % 4
        par3 = t % 3
        env.wait(eng, "vec", ("Ast", t))
        env.wait(eng, "sca", ("Acast", t - 4))
        i = eng.bn_aggr(T["bnag"][:, 2 * par:2 * par + 2],
                        T["bnst"][:, 12 * par3:12 * par3 + 12]) if E else None
        env.tick("vec", key=("Avar", t), inst=i)

    def _Af_prod(t):
        par = t % 4
        env.wait(eng, "vec", ("Avar", t))
        env.wait(eng, "sca", ("Acast", t))
        i = eng.scalar_tensor_tensor(T["macc"][:, 0:1], T["bnag"][:, 2 * par:2 * par + 1],
                                     T["rstdf"][:, par:par + 1], T["macc"][:, 0:1],
                                     op0=ALU.mult, op1=ALU.add) if E else None
        env.tick("vec", key=("Apr", t), inst=i)

    for s in range(16):
        _Ao_v(s)
    for j in range(16, NSEQ + 3):
        if j < NSEQ:
            _Af_stats(j)
        if 17 <= j <= NSEQ:
            _Af_aggr(j - 1)
        if 19 <= j <= NSEQ + 2:
            _Af_prod(j - 3)
    i = eng.tensor_copy(T["maccbf"][:, 0:1], T["macc"][:, 0:1]) if E else None
    env.tick("vec", key="maccbf", inst=i)
    env.wait(eng, "pe", ("Amm", NSEQ - 1))
    PBv = (P0, P1, P2, P3) if E else (None,) * 4
    env.wait(eng, "pe", "corr1")
    i = eng.tensor_copy(T["corrbf"][0:1, 0:1], P2[0:1, 300:301]) if E else None
    env.tick("vec", key="corrbf", inst=i)
    env.wait(eng, "pe", "corrP")
    i = eng.tensor_copy(T["corrc"][:, 0:1], P3[:, 300:301]) if E else None
    env.vbar(eng, i)
    i = None
    if E:
        for c in range(8):
            i = eng.tensor_scalar(T["scol"][:, c:c + 1],
                                  PBv[c // 2][:, (c % 2) * 512:(c % 2) * 512 + 1],
                                  T["corrc"][:, 0:1], None, op0=ALU.subtract)
    env.tick("vec", key="scol", inst=i)
    env.wait(eng, "pe", "vsum")
    i = None
    if E:
        for c in range(8):
            i = eng.tensor_copy(T["vcol"][:, c:c + 1],
                                PBv[c // 2][:, (c % 2) * 512:(c % 2) * 512 + 1])
    env.tick("vec", key="vcol", inst=i)
    env.wait(eng, "pe", "orow")
    i = eng.tensor_copy(T["orow"][0:1, :], P2[0:1, 0:1024]) if E else None
    env.vbar(eng, i)
    i = None
    if E:
        eng.bn_stats(T["bnst"][0:1, 0:6], T["orow"][0:1, 0:512])
        i = eng.bn_stats(T["bnst"][0:1, 6:12], T["orow"][0:1, 512:1024])
    env.vbar(eng, i)
    i = eng.bn_aggr(T["bnag"][0:1, 0:2], T["bnst"][0:1, 0:12]) if E else None
    env.vbar(eng, i)
    i = eng.tensor_scalar(T["smean"][0:1, 0:1], T["bnag"][0:1, 0:1],
                          -1.0, None, op0=ALU.mult) if E else None
    env.tick("vec", key="ovar", inst=i)
    if E:
        eng.wait_ge(env.sems["vec"], env.counts["vec"])
    env.wait(eng, "sca", "orstd")
    i = eng.tensor_scalar(T["orow"][0:1, :], T["orow"][0:1, :], T["smean"][0:1, 0:1],
                          T["srstd"][0:1, 0:1], op0=ALU.add, op1=ALU.mult) if E else None
    env.tick("vec", key="rrow", inst=i)
    env.wait(eng, "pe", "rbc")
    i = eng.tensor_copy(T["rbc"][:, :], P3[:, :]) if E else None
    env.tick("vec", key="rbc_sb", inst=i)

    def V1(t):
        big_t = T["big"][:, t * D:(t + 1) * D] if E else None
        i = eng.tensor_tensor(big_t, big_t, T["rbc"][:, :], op=ALU.add) if E else None
        env.tick("vec", key=("x1", t), inst=i)

    def V2(t):
        par = t % 2
        if t >= 2:
            env.wait(eng, "sca", ("B1rstd", t - 2))
        big_t = T["big"][:, t * D:(t + 1) * D] if E else None
        i = None
        if E:
            eng.bn_stats(T["bnst"][:, 0:6], big_t[:, 0:512])
            i = eng.bn_stats(T["bnst"][:, 6:12], big_t[:, 512:1024])
        env.vbar(eng, i)
        i = eng.bn_aggr(T["bnag"][:, 2 * par:2 * par + 2], T["bnst"][:, 0:12]) if E else None
        env.vbar(eng, i)
        i = eng.tensor_scalar(T["smean"][:, par:par + 1], T["bnag"][:, 2 * par:2 * par + 1],
                              -1.0, None, op0=ALU.mult) if E else None
        env.tick("vec", key=("B1var", t), inst=i)
        if E:
            eng.wait_ge(env.sems["vec"], env.counts["vec"])

    def V3(t):
        par = t % 2
        env.wait(eng, "sca", ("B1rstd", t))
        if t >= 2:
            env.wait(eng, "pe", ("tp", t - 2))
        big_t = T["big"][:, t * D:(t + 1) * D] if E else None
        i = eng.tensor_scalar(T["x2bf"][:, par * D:(par + 1) * D], big_t,
                              T["smean"][:, par:par + 1], T["srstd"][:, par:par + 1],
                              op0=ALU.add, op1=ALU.mult) if E else None
        env.tick("vec", key=("x2", t), inst=i)

    def V4(t):
        par = t % 2
        env.wait(eng, "pe", ("tp", t))
        pTPt = (P0, P1)[par] if E else None
        dst = T["x2T"][:, :].rearrange("p (c v) -> p c v", c=8)[:, :, t * 128:(t + 1) * 128] if E else None
        srcv = pTPt.rearrange("p (c v) -> p c v", c=8) if E else None
        i = eng.tensor_copy(dst, srcv) if E else None
        env.tick("vec", key=("x2Tevac", t), inst=i)

    def V5b(t):
        env.wait(eng, "sca", ("glog", t))
        esc = T["esc"][:, t * 16:(t + 1) * 16] if E else None
        i = eng.reduce_sum(T["m2"][:, 0:1], esc, axis=AX.X) if E else None
        env.vbar(eng, i)
        i = eng.reciprocal(T["rcp"][:, 0:1], T["m2"][:, 0:1]) if E else None
        env.vbar(eng, i)
        i = None
        if E:
            for g in range(4):
                i = eng.reduce_max(T["gbuf"][:, g:g + 1], esc[:, g * 4:(g + 1) * 4], axis=AX.X)
        env.vbar(eng, i)
        i = eng.max(out=T["gt8"][:, :], in_=T["gbuf"][:, :]) if E else None
        env.vbar(eng, i)
        i = None
        if E:
            for g in range(4):
                i = eng.tensor_scalar(T["gm"][:, g:g + 1], T["gbuf"][:, g:g + 1],
                                      T["gt8"][:, 1:2], None, op0=ALU.is_ge)
        env.vbar(eng, i)
        i = None
        if E:
            for g in range(4):
                i = eng.tensor_scalar(T["masked"][:, g * 4:(g + 1) * 4], esc[:, g * 4:(g + 1) * 4],
                                      T["gm"][:, g:g + 1], None, op0=ALU.mult)
        env.vbar(eng, i)
        i = eng.max(out=T["gt8"][:, :], in_=T["masked"][:, :]) if E else None
        env.vbar(eng, i)
        i = eng.tensor_scalar(T["maske"][:, :], T["masked"][:, :],
                              T["gt8"][:, 3:4], None, op0=ALU.is_ge) if E else None
        env.vbar(eng, i)
        i = eng.tensor_mul(T["masked"][:, :], esc, T["maske"][:, :]) if E else None
        env.vbar(eng, i)
        i = eng.tensor_scalar(T["c_all"][:, t * 16:(t + 1) * 16], T["masked"][:, :],
                              T["rcp"][:, 0:1], None, op0=ALU.mult) if E else None
        env.tick("vec", key=("gate", t), inst=i)

    stages = (V4, V3, V2, V1)
    for s in range(NTO + 3):
        for j, fn in enumerate(stages):
            t = s - (3 - j)
            if 0 <= t < NTO:
                fn(t)

    def F1(t):
        par = t % 2
        env.wait(eng, "vec", ("yacc", 4 * (17 * 4 + t // 4) + (t % 4)))
        if t >= 2:
            env.wait(eng, "sca", ("Frstd", t - 2))
        yv = T["big"][:, t * D:(t + 1) * D] if E else None
        i = None
        if E:
            eng.bn_stats(T["bnst"][:, 0:6], yv[:, 0:512])
            i = eng.bn_stats(T["bnst"][:, 6:12], yv[:, 512:1024])
        env.vbar(eng, i)
        i = eng.bn_aggr(T["bnag"][:, 2 * par:2 * par + 2], T["bnst"][:, 0:12]) if E else None
        env.vbar(eng, i)
        i = eng.tensor_scalar(T["smean"][:, par:par + 1], T["bnag"][:, 2 * par:2 * par + 1],
                              -1.0, None, op0=ALU.mult) if E else None
        env.tick("vec", key=("Fvar", t), inst=i)
        if E:
            eng.wait_ge(env.sems["vec"], env.counts["vec"])

    def F2(t):
        par = t % 2
        env.wait(eng, "sca", ("Frstd", t))
        if t >= 2:
            env.wait(eng, ("xin", "xin2")[t % 2], ("out", t - 2))
        env.wait(eng, ("wdma", "wdma2")[(t // 4) % 2], ("rdg", t // 4))
        yv = T["big"][:, t * D:(t + 1) * D] if E else None
        ost = T["outst"][:, par * D:(par + 1) * D] if E else None
        i = None
        if E:
            eng.tensor_scalar(ost, yv, T["smean"][:, par:par + 1],
                              T["srstd"][:, par:par + 1], op0=ALU.add, op1=ALU.mult)
            i = eng.tensor_add(ost, ost, T["x1rd"][:, (t % 4) * D:(t % 4) * D + D])
        env.tick("vec", key=("fin", t), inst=i)




    def _yacc(prev):
        up = prev // 4
        for blk in range(4):
            g = 4 * prev + blk
            tj = (prev % 4) * 4 + blk
            env.wait(eng, "pe", ("w2", g))
            pY = (P2, P3)[g % 2] if E else None
            yv = T["big"][:, tj * D:(tj + 1) * D] if E else None
            i = None
            if up == 0:
                env.wait(eng, "spill", ("spg", tj // 4))
                i = eng.tensor_copy(yv, pY[:, :]) if E else None
            elif up == 1:
                i = eng.tensor_add(yv, yv, pY[:, :]) if E else None
            else:
                ccol = T["c_all"][:, tj * 16 + (up - 2):tj * 16 + (up - 2) + 1] if E else None
                i = eng.scalar_tensor_tensor(yv, pY[:, :], ccol, yv,
                                             op0=ALU.mult, op1=ALU.add) if E else None
            env.tick("vec", key=("yacc", g), inst=i)

    fin_steps = []

    def _fin_step(s):
        if 0 <= s - 1 < NTO:
            F2(s - 1)
        if s < NTO:
            F1(s)

    for it2 in range(NIT2):
        ap = it2 % 2
        env.wait(eng, "pe", ("h3A", it2))
        env.wait(eng, "sca", ("siluA", it2))
        if it2 >= 2:
            env.wait(eng, "pe", ("w2", 4 * (it2 - 2) + 3))
        i = eng.tensor_tensor(T["abf"][:, ap * 2048:ap * 2048 + 1024],
                              T["s1"][:, 0:1024], P1[:, :], op=ALU.mult) if E else None
        env.tick("vec", key=("aA", it2), inst=i)
        env.wait(eng, "pe", ("h3B", it2))
        env.wait(eng, "sca", ("siluB", it2))
        i = eng.tensor_tensor(T["abf"][:, ap * 2048 + 1024:ap * 2048 + 2048],
                              T["s1"][:, 1024:2048], P1[:, :], op=ALU.mult) if E else None
        env.tick("vec", key=("aB", it2), inst=i)
        if it2 >= 1:
            _yacc(it2 - 1)
            prev = it2 - 1
            if prev >= 68:
                j = prev - 68
                for q in range(4):
                    _fin_step(4 * j + q)
        if it2 < 8:
            V5b(2 * it2)
            V5b(2 * it2 + 1)
    _yacc(NIT2 - 1)
    for s in range(12, NTO + 1):
        _fin_step(s)

def _xtile(T, t):
    if t < 16:
        return T["big"][:, t * D:(t + 1) * D]
    fi = t - 16
    return T["xff"][:, (fi % 8) * D:(fi % 8) * D + D]


def emit_scalar(env, eng, T):
    E = env.emit
    P0 = T["P0"] if E else None

    def SA2(t):
        par = t % 4
        env.wait(eng, "vec", ("Avar", t))
        if t >= 4:
            env.wait(eng, "pe", ("Amm", t - 4))
        i = None
        if E:
            eng.activation(T["lncol"][:, 0:1], T["bnag"][:, 2 * par + 1:2 * par + 2],
                           AF.Ln, bias=T["epscol"][:, 0:1])
            i = eng.activation(T["rcol"][:, par:par + 1], T["lncol"][:, 0:1], AF.Exp, scale=-0.5)
            if t >= 16:
                i = eng.activation(T["rstdf"][:, par:par + 1], T["lncol"][:, 0:1],
                                   AF.Exp, scale=-0.5)
        env.tick("sca", key=("Acast", t), inst=i)

    for s in range(NSEQ):
        SA2(s)
    env.wait(eng, "vec", "ovar")
    i = None
    if E:
        eng.activation(T["lncol"][0:1, 0:1], T["bnag"][0:1, 1:2], AF.Ln,
                       bias=T["epscol"][0:1, 0:1])
        i = eng.activation(T["srstd"][0:1, 0:1], T["lncol"][0:1, 0:1], AF.Exp, scale=-0.5)
    env.tick("sca", key="orstd", inst=i)

    def S2(t):
        par = t % 2
        env.wait(eng, "vec", ("B1var", t))
        i = None
        if E:
            eng.activation(T["lncol"][:, 0:1], T["bnag"][:, 2 * par + 1:2 * par + 2],
                           AF.Ln, bias=T["epscol"][:, 0:1])
            i = eng.activation(T["srstd"][:, par:par + 1], T["lncol"][:, 0:1], AF.Exp, scale=-0.5)
        env.tick("sca", key=("B1rstd", t), inst=i)

    def SE(t):
        par = t % 2
        env.wait(eng, "pe", ("gatemm", t))
        pGt = (T["P2"], T["P3"])[par] if E else None
        i = eng.activation(T["esc"][:, t * 16:(t + 1) * 16],
                           pGt[:, 0:16], AF.Exp) if E else None
        env.tick("sca", key=("glog", t), inst=i)

    for s in range(NTO + 4):
        for fn, lag in ((SE, 4), (S2, 0)):
            t = s - lag
            if 0 <= t < NTO:
                fn(t)

    def FS2(t):
        par = t % 2
        env.wait(eng, "vec", ("Fvar", t))
        i = None
        if E:
            eng.activation(T["lncol"][:, 0:1], T["bnag"][:, 2 * par + 1:2 * par + 2],
                           AF.Ln, bias=T["epscol"][:, 0:1])
            i = eng.activation(T["srstd"][:, par:par + 1], T["lncol"][:, 0:1], AF.Exp, scale=-0.5)
        env.tick("sca", key=("Frstd", t), inst=i)




    def _fs_step(s):
        if 0 <= s < NTO:
            FS2(s)

    for it2 in range(NIT2):
        env.wait(eng, "pe", ("h1A", it2))
        if it2 >= 1:
            env.wait(eng, "vec", ("aA", it2 - 1))
        i = eng.activation(T["s1"][:, 0:1024], P0[:, :], AF.Silu) if E else None
        env.tick("sca", key=("siluA", it2), inst=i)
        env.wait(eng, "pe", ("h1B", it2))
        if it2 >= 1:
            env.wait(eng, "vec", ("aB", it2 - 1))
        i = eng.activation(T["s1"][:, 1024:2048], P0[:, :], AF.Silu) if E else None
        env.tick("sca", key=("siluB", it2), inst=i)
        if it2 >= 69:
            j = it2 - 69
            for q in range(4):
                _fs_step(4 * j + q)
    for s in range(12, NTO):
        _fs_step(s)

def build_program():
    nc = bass.Bass("TRN2", target_bir_lowering=False, dynamic_dma_scratch_size=16384,
                   detect_race_conditions=False)
    T = {}
    T["x"] = nc.dram_tensor("x", [8192, D], F32, kind="ExternalInput")
    T["wvT"] = nc.dram_tensor("wvT", [D, D], F32, kind="ExternalInput")
    T["woT"] = nc.dram_tensor("woT", [D, D], F32, kind="ExternalInput")
    T["gate_wT"] = nc.dram_tensor("gate_wT", [D, 16], F32, kind="ExternalInput")
    T["w1"] = nc.dram_tensor("w1", [16, D, 512], F32, kind="ExternalInput")
    T["w2"] = nc.dram_tensor("w2", [16, 512, D], F32, kind="ExternalInput")
    T["w3"] = nc.dram_tensor("w3", [16, D, 512], F32, kind="ExternalInput")
    T["ws1"] = nc.dram_tensor("ws1", [D, D], F32, kind="ExternalInput")
    T["ws2"] = nc.dram_tensor("ws2", [D, D], F32, kind="ExternalInput")
    T["ws3"] = nc.dram_tensor("ws3", [D, D], F32, kind="ExternalInput")
    T["out"] = nc.dram_tensor("out", [TOK, D], F32, kind="ExternalOutput")
    T["x1spill"] = nc.dram_tensor("x1spill", [TOK, D], F32,
                                  kind="ExternalOutput" if DEBUG else "Internal")
    if DEBUG:
        T["dbg_y"] = nc.dram_tensor("dbg_y", [128, NTO * D], F32, kind="ExternalOutput")
        T["dbg_c"] = nc.dram_tensor("dbg_c", [128, NTO * 16], F32, kind="ExternalOutput")
        T["dbg_rbc"] = nc.dram_tensor("dbg_rbc", [128, D], F32, kind="ExternalOutput")
        T["dbg_scol"] = nc.dram_tensor("dbg_scol", [128, 8], BF16, kind="ExternalOutput")
        T["dbg_x2T"] = nc.dram_tensor("dbg_x2T", [128, 8 * TOK], BF16, kind="ExternalOutput")
        T["dbg_gwt"] = nc.dram_tensor("dbg_gwt", [128, 8 * 16], BF16, kind="ExternalOutput")
        T["dbg_orow"] = nc.dram_tensor("dbg_orow", [1, D], F32, kind="ExternalOutput")
        T["dbg_vcol"] = nc.dram_tensor("dbg_vcol", [128, 8], BF16, kind="ExternalOutput")
        T["dbg_esc"] = nc.dram_tensor("dbg_esc", [128, 32], F32, kind="ExternalOutput")
        T["dbg_scores"] = nc.dram_tensor("dbg_scores", [128, 16], F32, kind="ExternalOutput")

    with ExitStack() as ctx:
        e = ctx.enter_context

        def sb(name, shape, dt):
            h = e(nc.sbuf_tensor(name, shape, dt))
            T[name] = h
            return h

        sb("big", [128, NTO * D], F32)
        sb("x2T", [128, 8 * TOK], BF16)
        sb("wu1", [128, 2 * 4096], BF16)
        sb("wu3", [128, 2 * 4096], BF16)
        sb("wu2", [128, 2 * 4096], BF16)
        sb("gwt", [128, 8 * 16], BF16)
        sb("xff", [128, 10 * D], BF16)
        sb("x1rd", [128, 4 * D], BF16)
        sb("s1", [128, 2 * D], BF16)
        sb("abf", [128, 2 * 2048], BF16)
        sb("c_all", [128, NTO * 16], F32)
        sb("x2bf", [128, 2 * D], BF16)
        sb("ident", [128, 128], BF16)
        sb("outst", [128, 2 * D], F32)
        sb("rcp", [128, 1], F32)
        sb("bnst", [128, 36], F32)
        sb("bnag", [128, 8], F32)
        sb("rstdf", [128, 4], F32)
        sb("prodc", [128, 1], F32)
        sb("macc", [128, 1], F32)
        sb("maccbf", [128, 1], BF16)
        sb("corrc", [128, 1], F32)
        sb("corrbf", [1, 1], BF16)
        sb("onesbf", [1, 128], BF16)
        sb("onescol", [128, 1], BF16)
        sb("epscol", [128, 1], F32)
        sb("rbc", [128, D], BF16)
        sb("smean", [128, 4], F32)
        sb("srstd", [128, 4], F32)
        sb("m2", [128, 1], F32)
        sb("lncol", [128, 1], F32)
        sb("rcol", [128, 4], BF16)
        sb("scol", [128, 8], BF16)
        sb("vcol", [128, 8], BF16)
        sb("orow", [1, D], F32)
        sb("ones_row", [1, 128], F32)
        sb("scr", [128, 2 * 16], F32)
        sb("esc", [128, 16 * 16], F32)
        sb("gbuf", [128, 8], F32)
        sb("gt8", [128, 8], F32)
        sb("gm", [128, 4], F32)
        sb("masked", [128, 16], F32)
        sb("maske", [128, 16], F32)

        T["P0"] = e(nc.psum_tensor("P0", [128, 1024], F32))
        T["P1"] = e(nc.psum_tensor("P1", [128, 1024], F32))
        T["P2"] = e(nc.psum_tensor("P2", [128, 1024], F32))
        T["P3"] = e(nc.psum_tensor("P3", [128, 1024], F32))

        env = Env()
        emit_sync(env, None, T)
        emit_gpsimd(env, None, T)
        emit_tensor(env, None, T)
        emit_vector(env, None, T)
        emit_scalar(env, None, T)
        env.check_pass1()
        pass1_counts = dict(env.counts)

        env.emit = True
        env.reset()
        for s in SEMS:
            env.sems[s] = e(nc.semaphore(s))
        block = e(nc.Block())
        block.sync(lambda eng: emit_sync(env, eng, T))
        block.gpsimd(lambda eng: emit_gpsimd(env, eng, T))
        block.tensor(lambda eng: emit_tensor(env, eng, T))
        block.vector(lambda eng: emit_vector(env, eng, T))
        block.scalar(lambda eng: emit_scalar(env, eng, T))
        assert env.counts == pass1_counts, (env.counts, pass1_counts)
    return nc


_NC_CACHE = []


def kernel(L0_attn_x, L0_attn_w_qkv, L0_attn_w_o, L0_ffn_gate_w, L0_ffn_w1,
           L0_ffn_w2, L0_ffn_w3, L0_ffn_ws1, L0_ffn_ws2, L0_ffn_ws3):
    x = np.ascontiguousarray(np.asarray(L0_attn_x, dtype=np.float32))
    wvT = np.ascontiguousarray(np.asarray(L0_attn_w_qkv, dtype=np.float32)[2048:3072, :].T)
    woT = np.ascontiguousarray(np.asarray(L0_attn_w_o, dtype=np.float32).T)
    gate_wT = np.ascontiguousarray(np.asarray(L0_ffn_gate_w, dtype=np.float32).T)
    w1 = np.ascontiguousarray(np.asarray(L0_ffn_w1, dtype=np.float32))
    w2 = np.ascontiguousarray(np.asarray(L0_ffn_w2, dtype=np.float32))
    w3 = np.ascontiguousarray(np.asarray(L0_ffn_w3, dtype=np.float32))
    ws1 = np.ascontiguousarray(np.asarray(L0_ffn_ws1, dtype=np.float32))
    ws2 = np.ascontiguousarray(np.asarray(L0_ffn_ws2, dtype=np.float32))
    ws3 = np.ascontiguousarray(np.asarray(L0_ffn_ws3, dtype=np.float32))

    if not _NC_CACHE:
        _NC_CACHE.append(build_program())
    nc = _NC_CACHE[0]

    in_maps = []
    for core in range(8):
        b, q = divmod(core, 4)
        xb = x[b]
        x_seq = np.concatenate(
            [xb[q * TOK:(q + 1) * TOK], xb[:q * TOK], xb[(q + 1) * TOK:]], axis=0)
        in_maps.append({
            "x": np.ascontiguousarray(x_seq),
            "wvT": wvT, "woT": woT, "gate_wT": gate_wT,
            "w1": w1, "w2": w2, "w3": w3,
            "ws1": ws1, "ws2": ws2, "ws3": ws3,
        })
    res = run_bass_kernel_spmd(nc, in_maps, core_ids=list(range(8)))
    out = np.empty((2, 8192, D), dtype=np.float32)
    for core in range(8):
        b, q = divmod(core, 4)
        out[b, q * TOK:(q + 1) * TOK] = np.asarray(res.results[core]["out"])
    return out


# version 41
# speedup vs baseline: 1.0168x; 1.0052x over previous
import numpy as np
from contextlib import ExitStack

import concourse.bass as bass
import concourse.mybir as mybir
from concourse.bass_utils import run_bass_kernel_spmd

F32 = mybir.dt.float32
BF16 = mybir.dt.bfloat16
AF = mybir.ActivationFunctionType
ALU = mybir.AluOpType
AX = mybir.AxisListType

D = 1024
TOK = 2048
NTO = 16
NSEQ = 64
NUNITS = 18
NIT = NUNITS * 8
NIT2 = NUNITS * 4
EPS = 1e-5
NEG = -1.0e30

SEMS = ("xin", "xin2", "spill", "wdma", "wdma2", "gp", "pe", "vec", "sca")
DEBUG = False


class Env:
    def __init__(self):
        self.emit = False
        self.counts = {s: 0 for s in SEMS}
        self.ckpts = {}
        self.reqs = set()
        self.sems = {}

    def reset(self):
        self.counts = {s: 0 for s in SEMS}

    def tick(self, sem, key=None, n=1, inst=None):
        c = self.counts[sem] + n
        self.counts[sem] = c
        if key is not None:
            k = (sem, key)
            if self.emit:
                assert self.ckpts[k] == c, (k, self.ckpts[k], c)
            else:
                assert k not in self.ckpts, k
                self.ckpts[k] = c
        if self.emit and inst is not None:
            inst.then_inc(self.sems[sem], n)
        return c

    def at(self, sem, key):
        k = (sem, key)
        if not self.emit:
            self.reqs.add(k)
            return self.ckpts.get(k, 0)
        return self.ckpts[k]

    def wait(self, eng, sem, key):
        v = self.at(sem, key)
        if self.emit and v > 0:
            eng.wait_ge(self.sems[sem], v)

    def vbar(self, eng, inst):
        c = self.tick("vec", inst=inst)
        if self.emit:
            eng.wait_ge(self.sems["vec"], c)

    def check_pass1(self):
        missing = [k for k in self.reqs if k not in self.ckpts]
        assert not missing, f"unresolved ckpts: {sorted(missing)[:10]}"


def _unit_srcs(T, u):
    if u < 2:
        lo, hi = (0, 512) if u == 0 else (512, 1024)
        w1s = T["ws1"][:, :].rearrange("(k p) i -> p k i", p=128)[:, :, lo:hi]
        w3s = T["ws3"][:, :].rearrange("(k p) i -> p k i", p=128)[:, :, lo:hi]
        w2s = T["ws2"][lo:hi, :].rearrange("(k p) d -> p k d", p=128)
    else:
        e = u - 2
        w1s = T["w1"][e].rearrange("(k p) i -> p k i", p=128)
        w3s = T["w3"][e].rearrange("(k p) i -> p k i", p=128)
        w2s = T["w2"][e].rearrange("(k p) d -> p k d", p=128)
    return w1s, w3s, w2s


def emit_sync(env, eng, T):
    E = env.emit
    for t in range(16):
        g = t // 2
        sem = ("xin", "xin2")[g % 2]
        if t % 2 == 0 and g >= 2:
            if E:
                eng.wait_ge(env.sems[sem], env.at(sem, ("Axg", g - 2)))
        dst = T["big"][:, t * D:(t + 1) * D] if E else None
        i = eng.dma_start(out=dst, in_=T["x"][t * 128:(t + 1) * 128, :]) if E else None
        env.tick(sem, n=16, inst=i)
        if t % 2 == 1:
            env.tick(sem, key=("Axg", g), n=0)
    for t in range(NTO):
        if t % 4 == 0 and t > 0 and E:
            eng.wait_ge(env.sems["spill"], env.counts["spill"])
        env.wait(eng, "vec", ("x1", t))
        i = eng.dma_start(out=T["x1spill"][t * 128:(t + 1) * 128, :],
                          in_=T["big"][:, t * D:(t + 1) * D]) if E else None
        env.tick("spill", n=16, inst=i)
        if t % 4 == 3:
            env.tick("spill", key=("spg", t // 4), n=0)
    if DEBUG:
        env.wait(eng, "vec", ("fin", NTO - 1))
        for nm, src in (("dbg_y", T["big"][:, :]), ("dbg_c", T["c_all"][:, :]),
                        ("dbg_rbc", T["rbc"][:, :]), ("dbg_scol", T["scol"][:, :]),
                        ("dbg_x2T", T["x2T"][:, :]), ("dbg_gwt", T["gwt"][:, :]),
                        ("dbg_orow", T["orow"][0:1, :]), ("dbg_vcol", T["vcol"][:, :]),
                        ("dbg_esc", T["esc"][:, :]), ("dbg_scores", T["scores"][:, :])):
            i = eng.dma_start(out=T[nm][:, :], in_=src) if E else None
            env.tick("spill", n=16, inst=i)
    for t in range(NTO):
        osem = ("xin", "xin2")[t % 2]
        if t >= 2 and E:
            eng.wait_ge(env.sems[osem], env.at(osem, ("out", t - 2)))
        env.wait(eng, "vec", ("fin", t))
        i = eng.dma_start(out=T["out"][t * 128:(t + 1) * 128, :],
                          in_=T["outst"][:, (t % 2) * D:(t % 2) * D + D]) if E else None
        env.tick(osem, key=("out", t), n=16, inst=i)
    if E:
        eng.wait_ge(env.sems["spill"], env.counts["spill"])
        eng.wait_ge(env.sems["wdma"], env.counts["wdma"])
        eng.wait_ge(env.sems["wdma2"], env.counts["wdma2"])
        eng.wait_ge(env.sems["xin"], env.counts["xin"])
        eng.wait_ge(env.sems["xin2"], env.counts["xin2"])


def emit_gpsimd(env, eng, T):
    E = env.emit
    if E:
        eng.memset(T["ident"][:, :], 0.0)
    i = eng.affine_select(out=T["ident"][:, :], in_=T["ident"][:, :],
                          compare_op=ALU.not_equal, fill=1.0, base=0,
                          pattern=[[-1, 128]], channel_multiplier=1) if E else None
    env.tick("gp", key="ident", inst=i)
    def _fx2(t):
        fi = t - 16
        g = fi // 4
        sem = ("wdma", "wdma2")[g % 2]
        if fi % 4 == 0 and g >= 4:
            if E:
                eng.wait_ge(env.sems[sem], env.at(sem, ("Afg", g - 4)))
        prev = t - 9
        env.wait(eng, "vec", ("Acp", prev) if prev < 16 else ("Ast", prev))
        env.wait(eng, "pe", ("Amm", prev))
        for u in (t, t + 1):
            i = eng.dma_start(out=T["xff"][:, (u % 10) * D:(u % 10) * D + D],
                              in_=T["x"][u * 128:(u + 1) * 128, :]) if E else None
            env.tick(sem, n=16, inst=i)
        if fi % 4 == 2:
            env.tick(sem, key=("Afg", g), n=0)

    i = eng.dma_start(out=T["gwt"][:, :].rearrange("p (c e) -> p c e", c=8),
                      in_=T["gate_wT"][:, :].rearrange("(c p) e -> p c e", p=128)) if E else None
    env.tick("wdma", key=("w", "gate"), n=16, inst=i)
    i = eng.dma_start(out=T["wu1"][:, :].rearrange("p (c j) -> p c j", c=8),
                      in_=T["wvT"][:, :].rearrange("(c p) j -> p c j", p=128)) if E else None
    env.tick("wdma", key=("w", "wv"), n=16, inst=i)
    i = eng.dma_start(out=T["wu3"][:, :].rearrange("p (c j) -> p c j", c=8),
                      in_=T["woT"][:, :].rearrange("(c p) j -> p c j", p=128)) if E else None
    env.tick("wdma", key=("w", "wo"), n=16, inst=i)
    for t in range(16, NSEQ, 2):
        _fx2(t)
    if E:
        eng.wait_ge(env.sems["wdma"], env.counts["wdma"])
        eng.wait_ge(env.sems["wdma2"], env.counts["wdma2"])
    for u in range(NUNITS):
        p = u % 2
        if E:
            eng.wait_ge(env.sems["wdma"], env.counts["wdma"])
        if u == 0 or u == 1:
            env.wait(eng, "pe", "vsum")
        else:
            env.wait(eng, "pe", ("w2", 4 * ((u - 2) * 4 + 3) + 3))
        w1s, w3s, w2s = _unit_srcs(T, u) if E else (None, None, None)
        i = eng.dma_start(out=T["wu1"][:, p * 4096:(p + 1) * 4096].rearrange("p (k i) -> p k i", k=8),
                          in_=w1s) if E else None
        env.tick("wdma", n=16, inst=i)
        if u == 0 or u == 1:
            env.wait(eng, "pe", "orow")
        i = eng.dma_start(out=T["wu3"][:, p * 4096:(p + 1) * 4096].rearrange("p (k i) -> p k i", k=8),
                          in_=w3s) if E else None
        env.tick("wdma", n=16, inst=i)
        i = eng.dma_start(out=T["wu2"][:, p * 4096:(p + 1) * 4096].rearrange("p (k d) -> p k d", k=4),
                          in_=w2s) if E else None
        env.tick("wdma", key=("w", ("u", u)), n=16, inst=i)
    for t in range(NTO):
        g = t // 4
        rsem = ("wdma", "wdma2")[g % 2]
        if t % 4 == 0 and g >= 2 and E:
            eng.wait_ge(env.sems[rsem], env.at(rsem, ("rdg", g - 2)))
        if t >= 4:
            env.wait(eng, "vec", ("fin", t - 4))
        i = eng.dma_start(out=T["x1rd"][:, (t % 4) * D:(t % 4) * D + D],
                          in_=T["x1spill"][t * 128:(t + 1) * 128, :]) if E else None
        env.tick(rsem, n=16, inst=i)
        if t % 4 == 3:
            env.tick(rsem, key=("rdg", g), n=0)


def emit_tensor(env, eng, T):
    E = env.emit
    P0, P1, P2, P3 = (T["P0"], T["P1"], T["P2"], T["P3"]) if E else (None,) * 4
    PB = (P0, P1, P2, P3) if E else (None,) * 4
    for t in range(NSEQ):
        env.wait(eng, "sca", ("Acast", t))
        if t < 16:
            env.wait(eng, "vec", ("Acp", t))
        elif t % 4 == 0:
            env.wait(eng, ("wdma", "wdma2")[((t - 16) // 4) % 2], ("Afg", (t - 16) // 4))
        for c in range(8):
            dst = PB[c // 2][:, (c % 2) * 512:(c % 2) * 512 + 1] if E else None
            i = eng.matmul(dst,
                           T["xff"][:, (t % 10) * D + c * 128:(t % 10) * D + (c + 1) * 128],
                           T["rcol"][:, t % 4:t % 4 + 1],
                           start=(t == 0), stop=(t == NSEQ - 1)) if E else None
        env.tick("pe", key=("Amm", t), inst=i)
    env.wait(eng, "vec", "maccbf")
    i = eng.matmul(P2[0:1, 300:301], T["maccbf"][:, 0:1], T["onescol"][:, 0:1],
                   start=True, stop=True) if E else None
    env.tick("pe", key="corr1", inst=i)
    env.wait(eng, "vec", "corrbf")
    i = eng.matmul(P3[:, 300:301], T["onesbf"][0:1, 0:128], T["corrbf"][0:1, 0:1],
                   start=True, stop=True) if E else None
    env.tick("pe", key="corrP", inst=i)
    env.wait(eng, "vec", "scol")
    env.wait(eng, "wdma", ("w", "wo"))
    for jc in range(8):
        for dc in range(8):
            dst = PB[jc // 2][:, (jc % 2) * 512:(jc % 2) * 512 + 1] if E else None
            i = eng.matmul(dst,
                           T["wu1"][:, dc * 1024 + jc * 128:dc * 1024 + (jc + 1) * 128],
                           T["scol"][:, dc:dc + 1],
                           start=(dc == 0), stop=(dc == 7)) if E else None
    env.tick("pe", key="vsum", inst=i)
    env.wait(eng, "vec", "vcol")
    env.wait(eng, "wdma", ("w", "wo"))
    for c in range(8):
        for h in range(2):
            i = eng.matmul(P2[0:1, h * 512:(h + 1) * 512],
                           T["vcol"][:, c:c + 1],
                           T["wu3"][:, c * 1024 + h * 512:c * 1024 + (h + 1) * 512],
                           start=(c == 0), stop=(c == 7)) if E else None
    env.tick("pe", key="orow", inst=i)
    env.wait(eng, "vec", "rrow")
    i = eng.matmul(P3[:, 0:512], T["ones_row"][0:1, :], T["orow"][0:1, 0:512],
                   start=True, stop=True) if E else None
    i = eng.matmul(P3[:, 512:1024], T["ones_row"][0:1, :], T["orow"][0:1, 512:1024],
                   start=True, stop=True) if E else None
    env.tick("pe", key="rbc", inst=i)
    pTP = (P0, P1) if E else (None, None)
    pG = (P2, P3) if E else (None, None)
    env.wait(eng, "gp", "ident")
    env.wait(eng, "wdma", ("w", "gate"))
    for t in range(NTO):
        par = t % 2
        env.wait(eng, "vec", ("x2", t))
        if t >= 2:
            env.wait(eng, "vec", ("x2Tevac", t - 2))
        for c in range(8):
            i = eng.matmul(pTP[par][:, c * 128:(c + 1) * 128],
                           T["x2bf"][:, par * D + c * 128:par * D + (c + 1) * 128],
                           T["ident"][:, :], start=True, stop=True) if E else None
        env.tick("pe", key=("tp", t), inst=i)
        env.wait(eng, "vec", ("x2Tevac", t))
        if t >= 2:
            env.wait(eng, "sca", ("glog", t - 2))
        for c in range(8):
            i = eng.matmul(pG[par][:, 0:16],
                           T["x2T"][:, c * TOK + t * 128:c * TOK + (t + 1) * 128],
                           T["gwt"][:, c * 16:(c + 1) * 16],
                           start=(c == 0), stop=(c == 7)) if E else None
        env.tick("pe", key=("gatemm", t), inst=i)
    for it2 in range(NIT2):
        u, ntt = divmod(it2, 4)
        wp = u % 2
        if ntt == 0:
            env.wait(eng, "wdma", ("w", ("u", u)))
        if it2 == 0:
            env.wait(eng, "vec", ("x2Tevac", NTO - 1))
            env.wait(eng, "sca", ("glog", NTO - 1))
        if it2 >= 1:
            env.wait(eng, "sca", ("siluB", it2 - 1))
        for m in (0, 1):
            for k in range(8):
                i = eng.matmul(P0[:, m * 512:(m + 1) * 512],
                               T["wu1"][:, wp * 4096 + k * 512 + m * 128:wp * 4096 + k * 512 + (m + 1) * 128],
                               T["x2T"][:, k * TOK + ntt * 512:k * TOK + (ntt + 1) * 512],
                               start=(k == 0), stop=(k == 7)) if E else None
        env.tick("pe", key=("h1A", it2), inst=i)
        if it2 >= 1:
            env.wait(eng, "vec", ("aB", it2 - 1))
        for m in (0, 1):
            for k in range(8):
                i = eng.matmul(P1[:, m * 512:(m + 1) * 512],
                               T["wu3"][:, wp * 4096 + k * 512 + m * 128:wp * 4096 + k * 512 + (m + 1) * 128],
                               T["x2T"][:, k * TOK + ntt * 512:k * TOK + (ntt + 1) * 512],
                               start=(k == 0), stop=(k == 7)) if E else None
        env.tick("pe", key=("h3A", it2), inst=i)
        env.wait(eng, "sca", ("siluA", it2))
        for m in (2, 3):
            for k in range(8):
                i = eng.matmul(P0[:, (m - 2) * 512:(m - 1) * 512],
                               T["wu1"][:, wp * 4096 + k * 512 + m * 128:wp * 4096 + k * 512 + (m + 1) * 128],
                               T["x2T"][:, k * TOK + ntt * 512:k * TOK + (ntt + 1) * 512],
                               start=(k == 0), stop=(k == 7)) if E else None
        env.tick("pe", key=("h1B", it2), inst=i)
        env.wait(eng, "vec", ("aA", it2))
        for m in (2, 3):
            for k in range(8):
                i = eng.matmul(P1[:, (m - 2) * 512:(m - 1) * 512],
                               T["wu3"][:, wp * 4096 + k * 512 + m * 128:wp * 4096 + k * 512 + (m + 1) * 128],
                               T["x2T"][:, k * TOK + ntt * 512:k * TOK + (ntt + 1) * 512],
                               start=(k == 0), stop=(k == 7)) if E else None
        env.tick("pe", key=("h3B", it2), inst=i)
        if it2 >= 1:
            _emit_w2(env, eng, T, it2 - 1)
    _emit_w2(env, eng, T, NIT2 - 1)


def _emit_w2(env, eng, T, prev):
    E = env.emit
    up = prev // 4
    wp = up % 2
    ap = prev % 2
    for blk in range(4):
        g = 4 * prev + blk
        if g >= 2:
            env.wait(eng, "vec", ("yacc", g - 2))
        pY = (T["P2"], T["P3"])[g % 2] if E else None
        for dh in range(2):
            for k in range(4):
                i = eng.matmul(pY[:, dh * 512:(dh + 1) * 512],
                               T["abf"][:, ap * 2048 + k * 512 + blk * 128:ap * 2048 + k * 512 + (blk + 1) * 128],
                               T["wu2"][:, wp * 4096 + k * 1024 + dh * 512:wp * 4096 + k * 1024 + (dh + 1) * 512],
                               start=(k == 0), stop=(k == 3)) if E else None
        env.tick("pe", key=("w2", g), inst=i)


def emit_vector(env, eng, T):
    E = env.emit
    P0, P1, P2, P3 = (T["P0"], T["P1"], T["P2"], T["P3"]) if E else (None,) * 4
    if E:
        eng.memset(T["ones_row"][0:1, :], 1.0)
        eng.memset(T["onesbf"][0:1, :], 1.0)
        eng.memset(T["onescol"][:, 0:1], 1.0)
        eng.memset(T["macc"][:, 0:1], 0.0)
        eng.memset(T["gbuf"][:, 4:8], NEG)
        eng.memset(T["epscol"][:, 0:1], EPS)
    def _Ao_v(t):
        par = t % 4
        xt = T["big"][:, t * D:(t + 1) * D] if E else None
        g = t // 2
        env.wait(eng, ("xin", "xin2")[g % 2], ("Axg", g))
        if t >= 4:
            env.wait(eng, "sca", ("Acast", t - 4))
        i = None
        if E:
            eng.bn_stats(T["bnst"][:, 0:6], xt[:, 0:512])
            i = eng.bn_stats(T["bnst"][:, 6:12], xt[:, 512:1024])
        env.vbar(eng, i)
        i = eng.bn_aggr(T["bnag"][:, 2 * par:2 * par + 2], T["bnst"][:, 0:12]) if E else None
        env.vbar(eng, i)
        i = eng.tensor_scalar(T["smean"][:, par:par + 1], T["bnag"][:, 2 * par:2 * par + 1],
                              -1.0, None, op0=ALU.mult) if E else None
        env.tick("vec", key=("Avar", t), inst=i)
        if E:
            eng.wait_ge(env.sems["vec"], env.counts["vec"])
        if t >= 10:
            env.wait(eng, "pe", ("Amm", t - 10))
        i = eng.tensor_scalar(T["xff"][:, (t % 10) * D:(t % 10 + 1) * D], xt,
                              T["smean"][:, par:par + 1], None, op0=ALU.add) if E else None
        env.tick("vec", key=("Acp", t), inst=i)

    def _Af_stats(t):
        if t % 4 == 0:
            g = (t - 16) // 4
            env.wait(eng, ("wdma", "wdma2")[g % 2], ("Afg", g))
        if t >= 19:
            env.wait(eng, "vec", ("Avar", t - 3))
        par3 = t % 3
        xt = T["xff"][:, (t % 10) * D:(t % 10 + 1) * D] if E else None
        i = None
        if E:
            eng.bn_stats(T["bnst"][:, 12 * par3:12 * par3 + 6], xt[:, 0:512])
            i = eng.bn_stats(T["bnst"][:, 12 * par3 + 6:12 * par3 + 12], xt[:, 512:1024])
        env.tick("vec", key=("Ast", t), inst=i)

    def _Af_aggr(t):
        par = t % 4
        par3 = t % 3
        env.wait(eng, "vec", ("Ast", t))
        env.wait(eng, "sca", ("Acast", t - 4))
        i = eng.bn_aggr(T["bnag"][:, 2 * par:2 * par + 2],
                        T["bnst"][:, 12 * par3:12 * par3 + 12]) if E else None
        env.tick("vec", key=("Avar", t), inst=i)

    def _Af_prod(t):
        par = t % 4
        env.wait(eng, "vec", ("Avar", t))
        env.wait(eng, "sca", ("Acast", t))
        i = eng.scalar_tensor_tensor(T["macc"][:, 0:1], T["bnag"][:, 2 * par:2 * par + 1],
                                     T["rstdf"][:, par:par + 1], T["macc"][:, 0:1],
                                     op0=ALU.mult, op1=ALU.add) if E else None
        env.tick("vec", key=("Apr", t), inst=i)

    for s in range(16):
        _Ao_v(s)
    for j in range(16, NSEQ + 3):
        if j < NSEQ:
            _Af_stats(j)
        if 17 <= j <= NSEQ:
            _Af_aggr(j - 1)
        if 19 <= j <= NSEQ + 2:
            _Af_prod(j - 3)
    i = eng.tensor_copy(T["maccbf"][:, 0:1], T["macc"][:, 0:1]) if E else None
    env.tick("vec", key="maccbf", inst=i)
    env.wait(eng, "pe", ("Amm", NSEQ - 1))
    PBv = (P0, P1, P2, P3) if E else (None,) * 4
    env.wait(eng, "pe", "corr1")
    i = eng.tensor_copy(T["corrbf"][0:1, 0:1], P2[0:1, 300:301]) if E else None
    env.tick("vec", key="corrbf", inst=i)
    env.wait(eng, "pe", "corrP")
    i = eng.tensor_copy(T["corrc"][:, 0:1], P3[:, 300:301]) if E else None
    env.vbar(eng, i)
    i = None
    if E:
        for c in range(8):
            i = eng.tensor_scalar(T["scol"][:, c:c + 1],
                                  PBv[c // 2][:, (c % 2) * 512:(c % 2) * 512 + 1],
                                  T["corrc"][:, 0:1], None, op0=ALU.subtract)
    env.tick("vec", key="scol", inst=i)
    env.wait(eng, "pe", "vsum")
    i = None
    if E:
        for c in range(8):
            i = eng.tensor_copy(T["vcol"][:, c:c + 1],
                                PBv[c // 2][:, (c % 2) * 512:(c % 2) * 512 + 1])
    env.tick("vec", key="vcol", inst=i)
    env.wait(eng, "pe", "orow")
    i = eng.tensor_copy(T["orow"][0:1, :], P2[0:1, 0:1024]) if E else None
    env.vbar(eng, i)
    i = None
    if E:
        eng.bn_stats(T["bnst"][0:1, 0:6], T["orow"][0:1, 0:512])
        i = eng.bn_stats(T["bnst"][0:1, 6:12], T["orow"][0:1, 512:1024])
    env.vbar(eng, i)
    i = eng.bn_aggr(T["bnag"][0:1, 0:2], T["bnst"][0:1, 0:12]) if E else None
    env.vbar(eng, i)
    i = eng.tensor_scalar(T["smean"][0:1, 0:1], T["bnag"][0:1, 0:1],
                          -1.0, None, op0=ALU.mult) if E else None
    env.tick("vec", key="ovar", inst=i)
    if E:
        eng.wait_ge(env.sems["vec"], env.counts["vec"])
    env.wait(eng, "sca", "orstd")
    i = eng.tensor_scalar(T["orow"][0:1, :], T["orow"][0:1, :], T["smean"][0:1, 0:1],
                          T["srstd"][0:1, 0:1], op0=ALU.add, op1=ALU.mult) if E else None
    env.tick("vec", key="rrow", inst=i)
    env.wait(eng, "pe", "rbc")
    i = eng.tensor_copy(T["rbc"][:, :], P3[:, :]) if E else None
    env.tick("vec", key="rbc_sb", inst=i)

    def V1(t):
        big_t = T["big"][:, t * D:(t + 1) * D] if E else None
        i = eng.tensor_tensor(big_t, big_t, T["rbc"][:, :], op=ALU.add) if E else None
        env.tick("vec", key=("x1", t), inst=i)

    def V2(t):
        if t >= 3:
            env.wait(eng, "vec", ("B1var", t - 3))
        par3 = t % 3
        big_t = T["big"][:, t * D:(t + 1) * D] if E else None
        i = None
        if E:
            eng.bn_stats(T["bnst"][:, 12 * par3:12 * par3 + 6], big_t[:, 0:512])
            i = eng.bn_stats(T["bnst"][:, 12 * par3 + 6:12 * par3 + 12], big_t[:, 512:1024])
        env.tick("vec", key=("B1st", t), inst=i)

    def V2a(t):
        par = t % 2
        par3 = t % 3
        env.wait(eng, "vec", ("B1st", t))
        if t >= 2:
            env.wait(eng, "sca", ("B1rstd", t - 2))
        i = eng.bn_aggr(T["bnag"][:, 2 * par:2 * par + 2],
                        T["bnst"][:, 12 * par3:12 * par3 + 12]) if E else None
        env.tick("vec", key=("B1var", t), inst=i)

    def V2n(t):
        par = t % 2
        env.wait(eng, "vec", ("B1var", t))
        i = eng.tensor_scalar(T["smean"][:, par:par + 1], T["bnag"][:, 2 * par:2 * par + 1],
                              -1.0, None, op0=ALU.mult) if E else None
        env.tick("vec", key=("B1nm", t), inst=i)

    def V3(t):
        par = t % 2
        env.wait(eng, "vec", ("B1nm", t))
        env.wait(eng, "sca", ("B1rstd", t))
        if t >= 2:
            env.wait(eng, "pe", ("tp", t - 2))
        big_t = T["big"][:, t * D:(t + 1) * D] if E else None
        i = eng.tensor_scalar(T["x2bf"][:, par * D:(par + 1) * D], big_t,
                              T["smean"][:, par:par + 1], T["srstd"][:, par:par + 1],
                              op0=ALU.add, op1=ALU.mult) if E else None
        env.tick("vec", key=("x2", t), inst=i)

    def V4(t):
        par = t % 2
        env.wait(eng, "pe", ("tp", t))
        pTPt = (P0, P1)[par] if E else None
        dst = T["x2T"][:, :].rearrange("p (c v) -> p c v", c=8)[:, :, t * 128:(t + 1) * 128] if E else None
        srcv = pTPt.rearrange("p (c v) -> p c v", c=8) if E else None
        i = eng.tensor_copy(dst, srcv) if E else None
        env.tick("vec", key=("x2Tevac", t), inst=i)

    def V5b(t):
        env.wait(eng, "sca", ("glog", t))
        esc = T["esc"][:, t * 16:(t + 1) * 16] if E else None
        i = eng.reduce_sum(T["m2"][:, 0:1], esc, axis=AX.X) if E else None
        env.vbar(eng, i)
        i = eng.reciprocal(T["rcp"][:, 0:1], T["m2"][:, 0:1]) if E else None
        env.vbar(eng, i)
        i = None
        if E:
            for g in range(4):
                i = eng.reduce_max(T["gbuf"][:, g:g + 1], esc[:, g * 4:(g + 1) * 4], axis=AX.X)
        env.vbar(eng, i)
        i = eng.max(out=T["gt8"][:, :], in_=T["gbuf"][:, :]) if E else None
        env.vbar(eng, i)
        i = None
        if E:
            for g in range(4):
                i = eng.tensor_scalar(T["gm"][:, g:g + 1], T["gbuf"][:, g:g + 1],
                                      T["gt8"][:, 1:2], None, op0=ALU.is_ge)
        env.vbar(eng, i)
        i = None
        if E:
            for g in range(4):
                i = eng.tensor_scalar(T["masked"][:, g * 4:(g + 1) * 4], esc[:, g * 4:(g + 1) * 4],
                                      T["gm"][:, g:g + 1], None, op0=ALU.mult)
        env.vbar(eng, i)
        i = eng.max(out=T["gt8"][:, :], in_=T["masked"][:, :]) if E else None
        env.vbar(eng, i)
        i = eng.tensor_scalar(T["maske"][:, :], T["masked"][:, :],
                              T["gt8"][:, 3:4], None, op0=ALU.is_ge) if E else None
        env.vbar(eng, i)
        i = eng.tensor_mul(T["masked"][:, :], esc, T["maske"][:, :]) if E else None
        env.vbar(eng, i)
        i = eng.tensor_scalar(T["c_all"][:, t * 16:(t + 1) * 16], T["masked"][:, :],
                              T["rcp"][:, 0:1], None, op0=ALU.mult) if E else None
        env.tick("vec", key=("gate", t), inst=i)

    stages = (V4, V3, V2n, V2a, V2, V1)
    for s in range(NTO + 5):
        for j, fn in enumerate(stages):
            t = s - (5 - j)
            if 0 <= t < NTO:
                fn(t)

    def F1(t):
        par = t % 2
        env.wait(eng, "vec", ("yacc", 4 * (17 * 4 + t // 4) + (t % 4)))
        if t >= 2:
            env.wait(eng, "sca", ("Frstd", t - 2))
        yv = T["big"][:, t * D:(t + 1) * D] if E else None
        i = None
        if E:
            eng.bn_stats(T["bnst"][:, 0:6], yv[:, 0:512])
            i = eng.bn_stats(T["bnst"][:, 6:12], yv[:, 512:1024])
        env.vbar(eng, i)
        i = eng.bn_aggr(T["bnag"][:, 2 * par:2 * par + 2], T["bnst"][:, 0:12]) if E else None
        env.vbar(eng, i)
        i = eng.tensor_scalar(T["smean"][:, par:par + 1], T["bnag"][:, 2 * par:2 * par + 1],
                              -1.0, None, op0=ALU.mult) if E else None
        env.tick("vec", key=("Fvar", t), inst=i)
        if E:
            eng.wait_ge(env.sems["vec"], env.counts["vec"])

    def F2(t):
        par = t % 2
        env.wait(eng, "sca", ("Frstd", t))
        if t >= 2:
            env.wait(eng, ("xin", "xin2")[t % 2], ("out", t - 2))
        env.wait(eng, ("wdma", "wdma2")[(t // 4) % 2], ("rdg", t // 4))
        yv = T["big"][:, t * D:(t + 1) * D] if E else None
        ost = T["outst"][:, par * D:(par + 1) * D] if E else None
        i = None
        if E:
            eng.tensor_scalar(ost, yv, T["smean"][:, par:par + 1],
                              T["srstd"][:, par:par + 1], op0=ALU.add, op1=ALU.mult)
            i = eng.tensor_add(ost, ost, T["x1rd"][:, (t % 4) * D:(t % 4) * D + D])
        env.tick("vec", key=("fin", t), inst=i)




    def _yacc(prev):
        up = prev // 4
        for blk in range(4):
            g = 4 * prev + blk
            tj = (prev % 4) * 4 + blk
            env.wait(eng, "pe", ("w2", g))
            pY = (P2, P3)[g % 2] if E else None
            yv = T["big"][:, tj * D:(tj + 1) * D] if E else None
            i = None
            if up == 0:
                env.wait(eng, "spill", ("spg", tj // 4))
                i = eng.tensor_copy(yv, pY[:, :]) if E else None
            elif up == 1:
                i = eng.tensor_add(yv, yv, pY[:, :]) if E else None
            else:
                ccol = T["c_all"][:, tj * 16 + (up - 2):tj * 16 + (up - 2) + 1] if E else None
                i = eng.scalar_tensor_tensor(yv, pY[:, :], ccol, yv,
                                             op0=ALU.mult, op1=ALU.add) if E else None
            env.tick("vec", key=("yacc", g), inst=i)

    fin_steps = []

    def _fin_step(s):
        if 0 <= s - 1 < NTO:
            F2(s - 1)
        if s < NTO:
            F1(s)

    for it2 in range(NIT2):
        ap = it2 % 2
        env.wait(eng, "pe", ("h3A", it2))
        env.wait(eng, "sca", ("siluA", it2))
        if it2 >= 2:
            env.wait(eng, "pe", ("w2", 4 * (it2 - 2) + 3))
        i = eng.tensor_tensor(T["abf"][:, ap * 2048:ap * 2048 + 1024],
                              T["s1"][:, 0:1024], P1[:, :], op=ALU.mult) if E else None
        env.tick("vec", key=("aA", it2), inst=i)
        env.wait(eng, "pe", ("h3B", it2))
        env.wait(eng, "sca", ("siluB", it2))
        i = eng.tensor_tensor(T["abf"][:, ap * 2048 + 1024:ap * 2048 + 2048],
                              T["s1"][:, 1024:2048], P1[:, :], op=ALU.mult) if E else None
        env.tick("vec", key=("aB", it2), inst=i)
        if it2 >= 1:
            _yacc(it2 - 1)
            prev = it2 - 1
            if prev >= 68:
                j = prev - 68
                for q in range(4):
                    _fin_step(4 * j + q)
        if it2 < 8:
            V5b(2 * it2)
            V5b(2 * it2 + 1)
    _yacc(NIT2 - 1)
    for s in range(12, NTO + 1):
        _fin_step(s)

def _xtile(T, t):
    if t < 16:
        return T["big"][:, t * D:(t + 1) * D]
    fi = t - 16
    return T["xff"][:, (fi % 8) * D:(fi % 8) * D + D]


def emit_scalar(env, eng, T):
    E = env.emit
    P0 = T["P0"] if E else None

    def SA2(t):
        par = t % 4
        env.wait(eng, "vec", ("Avar", t))
        if t >= 4:
            env.wait(eng, "pe", ("Amm", t - 4))
        i = None
        if E:
            eng.activation(T["lncol"][:, 0:1], T["bnag"][:, 2 * par + 1:2 * par + 2],
                           AF.Ln, bias=T["epscol"][:, 0:1])
            i = eng.activation(T["rcol"][:, par:par + 1], T["lncol"][:, 0:1], AF.Exp, scale=-0.5)
            if t >= 16:
                i = eng.activation(T["rstdf"][:, par:par + 1], T["lncol"][:, 0:1],
                                   AF.Exp, scale=-0.5)
        env.tick("sca", key=("Acast", t), inst=i)

    for s in range(NSEQ):
        SA2(s)
    env.wait(eng, "vec", "ovar")
    i = None
    if E:
        eng.activation(T["lncol"][0:1, 0:1], T["bnag"][0:1, 1:2], AF.Ln,
                       bias=T["epscol"][0:1, 0:1])
        i = eng.activation(T["srstd"][0:1, 0:1], T["lncol"][0:1, 0:1], AF.Exp, scale=-0.5)
    env.tick("sca", key="orstd", inst=i)

    def S2(t):
        par = t % 2
        env.wait(eng, "vec", ("B1var", t))
        i = None
        if E:
            eng.activation(T["lncol"][:, 0:1], T["bnag"][:, 2 * par + 1:2 * par + 2],
                           AF.Ln, bias=T["epscol"][:, 0:1])
            i = eng.activation(T["srstd"][:, par:par + 1], T["lncol"][:, 0:1], AF.Exp, scale=-0.5)
        env.tick("sca", key=("B1rstd", t), inst=i)

    def SE(t):
        par = t % 2
        env.wait(eng, "pe", ("gatemm", t))
        pGt = (T["P2"], T["P3"])[par] if E else None
        i = eng.activation(T["esc"][:, t * 16:(t + 1) * 16],
                           pGt[:, 0:16], AF.Exp) if E else None
        env.tick("sca", key=("glog", t), inst=i)

    for s in range(NTO + 4):
        for fn, lag in ((SE, 4), (S2, 0)):
            t = s - lag
            if 0 <= t < NTO:
                fn(t)

    def FS2(t):
        par = t % 2
        env.wait(eng, "vec", ("Fvar", t))
        i = None
        if E:
            eng.activation(T["lncol"][:, 0:1], T["bnag"][:, 2 * par + 1:2 * par + 2],
                           AF.Ln, bias=T["epscol"][:, 0:1])
            i = eng.activation(T["srstd"][:, par:par + 1], T["lncol"][:, 0:1], AF.Exp, scale=-0.5)
        env.tick("sca", key=("Frstd", t), inst=i)




    def _fs_step(s):
        if 0 <= s < NTO:
            FS2(s)

    for it2 in range(NIT2):
        env.wait(eng, "pe", ("h1A", it2))
        if it2 >= 1:
            env.wait(eng, "vec", ("aA", it2 - 1))
        i = eng.activation(T["s1"][:, 0:1024], P0[:, :], AF.Silu) if E else None
        env.tick("sca", key=("siluA", it2), inst=i)
        env.wait(eng, "pe", ("h1B", it2))
        if it2 >= 1:
            env.wait(eng, "vec", ("aB", it2 - 1))
        i = eng.activation(T["s1"][:, 1024:2048], P0[:, :], AF.Silu) if E else None
        env.tick("sca", key=("siluB", it2), inst=i)
        if it2 >= 69:
            j = it2 - 69
            for q in range(4):
                _fs_step(4 * j + q)
    for s in range(12, NTO):
        _fs_step(s)

def build_program():
    nc = bass.Bass("TRN2", target_bir_lowering=False, dynamic_dma_scratch_size=16384,
                   detect_race_conditions=False)
    T = {}
    T["x"] = nc.dram_tensor("x", [8192, D], F32, kind="ExternalInput")
    T["wvT"] = nc.dram_tensor("wvT", [D, D], F32, kind="ExternalInput")
    T["woT"] = nc.dram_tensor("woT", [D, D], F32, kind="ExternalInput")
    T["gate_wT"] = nc.dram_tensor("gate_wT", [D, 16], F32, kind="ExternalInput")
    T["w1"] = nc.dram_tensor("w1", [16, D, 512], F32, kind="ExternalInput")
    T["w2"] = nc.dram_tensor("w2", [16, 512, D], F32, kind="ExternalInput")
    T["w3"] = nc.dram_tensor("w3", [16, D, 512], F32, kind="ExternalInput")
    T["ws1"] = nc.dram_tensor("ws1", [D, D], F32, kind="ExternalInput")
    T["ws2"] = nc.dram_tensor("ws2", [D, D], F32, kind="ExternalInput")
    T["ws3"] = nc.dram_tensor("ws3", [D, D], F32, kind="ExternalInput")
    T["out"] = nc.dram_tensor("out", [TOK, D], F32, kind="ExternalOutput")
    T["x1spill"] = nc.dram_tensor("x1spill", [TOK, D], F32,
                                  kind="ExternalOutput" if DEBUG else "Internal")
    if DEBUG:
        T["dbg_y"] = nc.dram_tensor("dbg_y", [128, NTO * D], F32, kind="ExternalOutput")
        T["dbg_c"] = nc.dram_tensor("dbg_c", [128, NTO * 16], F32, kind="ExternalOutput")
        T["dbg_rbc"] = nc.dram_tensor("dbg_rbc", [128, D], F32, kind="ExternalOutput")
        T["dbg_scol"] = nc.dram_tensor("dbg_scol", [128, 8], BF16, kind="ExternalOutput")
        T["dbg_x2T"] = nc.dram_tensor("dbg_x2T", [128, 8 * TOK], BF16, kind="ExternalOutput")
        T["dbg_gwt"] = nc.dram_tensor("dbg_gwt", [128, 8 * 16], BF16, kind="ExternalOutput")
        T["dbg_orow"] = nc.dram_tensor("dbg_orow", [1, D], F32, kind="ExternalOutput")
        T["dbg_vcol"] = nc.dram_tensor("dbg_vcol", [128, 8], BF16, kind="ExternalOutput")
        T["dbg_esc"] = nc.dram_tensor("dbg_esc", [128, 32], F32, kind="ExternalOutput")
        T["dbg_scores"] = nc.dram_tensor("dbg_scores", [128, 16], F32, kind="ExternalOutput")

    with ExitStack() as ctx:
        e = ctx.enter_context

        def sb(name, shape, dt):
            h = e(nc.sbuf_tensor(name, shape, dt))
            T[name] = h
            return h

        sb("big", [128, NTO * D], F32)
        sb("x2T", [128, 8 * TOK], BF16)
        sb("wu1", [128, 2 * 4096], BF16)
        sb("wu3", [128, 2 * 4096], BF16)
        sb("wu2", [128, 2 * 4096], BF16)
        sb("gwt", [128, 8 * 16], BF16)
        sb("xff", [128, 10 * D], BF16)
        sb("x1rd", [128, 4 * D], BF16)
        sb("s1", [128, 2 * D], BF16)
        sb("abf", [128, 2 * 2048], BF16)
        sb("c_all", [128, NTO * 16], F32)
        sb("x2bf", [128, 2 * D], BF16)
        sb("ident", [128, 128], BF16)
        sb("outst", [128, 2 * D], F32)
        sb("rcp", [128, 1], F32)
        sb("bnst", [128, 36], F32)
        sb("bnag", [128, 8], F32)
        sb("rstdf", [128, 4], F32)
        sb("prodc", [128, 1], F32)
        sb("macc", [128, 1], F32)
        sb("maccbf", [128, 1], BF16)
        sb("corrc", [128, 1], F32)
        sb("corrbf", [1, 1], BF16)
        sb("onesbf", [1, 128], BF16)
        sb("onescol", [128, 1], BF16)
        sb("epscol", [128, 1], F32)
        sb("rbc", [128, D], BF16)
        sb("smean", [128, 4], F32)
        sb("srstd", [128, 4], F32)
        sb("m2", [128, 1], F32)
        sb("lncol", [128, 1], F32)
        sb("rcol", [128, 4], BF16)
        sb("scol", [128, 8], BF16)
        sb("vcol", [128, 8], BF16)
        sb("orow", [1, D], F32)
        sb("ones_row", [1, 128], F32)
        sb("scr", [128, 2 * 16], F32)
        sb("esc", [128, 16 * 16], F32)
        sb("gbuf", [128, 8], F32)
        sb("gt8", [128, 8], F32)
        sb("gm", [128, 4], F32)
        sb("masked", [128, 16], F32)
        sb("maske", [128, 16], F32)

        T["P0"] = e(nc.psum_tensor("P0", [128, 1024], F32))
        T["P1"] = e(nc.psum_tensor("P1", [128, 1024], F32))
        T["P2"] = e(nc.psum_tensor("P2", [128, 1024], F32))
        T["P3"] = e(nc.psum_tensor("P3", [128, 1024], F32))

        env = Env()
        emit_sync(env, None, T)
        emit_gpsimd(env, None, T)
        emit_tensor(env, None, T)
        emit_vector(env, None, T)
        emit_scalar(env, None, T)
        env.check_pass1()
        pass1_counts = dict(env.counts)

        env.emit = True
        env.reset()
        for s in SEMS:
            env.sems[s] = e(nc.semaphore(s))
        block = e(nc.Block())
        block.sync(lambda eng: emit_sync(env, eng, T))
        block.gpsimd(lambda eng: emit_gpsimd(env, eng, T))
        block.tensor(lambda eng: emit_tensor(env, eng, T))
        block.vector(lambda eng: emit_vector(env, eng, T))
        block.scalar(lambda eng: emit_scalar(env, eng, T))
        assert env.counts == pass1_counts, (env.counts, pass1_counts)
    return nc


_NC_CACHE = []


def kernel(L0_attn_x, L0_attn_w_qkv, L0_attn_w_o, L0_ffn_gate_w, L0_ffn_w1,
           L0_ffn_w2, L0_ffn_w3, L0_ffn_ws1, L0_ffn_ws2, L0_ffn_ws3):
    x = np.ascontiguousarray(np.asarray(L0_attn_x, dtype=np.float32))
    wvT = np.ascontiguousarray(np.asarray(L0_attn_w_qkv, dtype=np.float32)[2048:3072, :].T)
    woT = np.ascontiguousarray(np.asarray(L0_attn_w_o, dtype=np.float32).T)
    gate_wT = np.ascontiguousarray(np.asarray(L0_ffn_gate_w, dtype=np.float32).T)
    w1 = np.ascontiguousarray(np.asarray(L0_ffn_w1, dtype=np.float32))
    w2 = np.ascontiguousarray(np.asarray(L0_ffn_w2, dtype=np.float32))
    w3 = np.ascontiguousarray(np.asarray(L0_ffn_w3, dtype=np.float32))
    ws1 = np.ascontiguousarray(np.asarray(L0_ffn_ws1, dtype=np.float32))
    ws2 = np.ascontiguousarray(np.asarray(L0_ffn_ws2, dtype=np.float32))
    ws3 = np.ascontiguousarray(np.asarray(L0_ffn_ws3, dtype=np.float32))

    if not _NC_CACHE:
        _NC_CACHE.append(build_program())
    nc = _NC_CACHE[0]

    in_maps = []
    for core in range(8):
        b, q = divmod(core, 4)
        xb = x[b]
        x_seq = np.concatenate(
            [xb[q * TOK:(q + 1) * TOK], xb[:q * TOK], xb[(q + 1) * TOK:]], axis=0)
        in_maps.append({
            "x": np.ascontiguousarray(x_seq),
            "wvT": wvT, "woT": woT, "gate_wT": gate_wT,
            "w1": w1, "w2": w2, "w3": w3,
            "ws1": ws1, "ws2": ws2, "ws3": ws3,
        })
    res = run_bass_kernel_spmd(nc, in_maps, core_ids=list(range(8)))
    out = np.empty((2, 8192, D), dtype=np.float32)
    for core in range(8):
        b, q = divmod(core, 4)
        out[b, q * TOK:(q + 1) * TOK] = np.asarray(res.results[core]["out"])
    return out
